# version 10
# baseline (speedup 1.0000x reference)
import numpy as np
import ml_dtypes
from contextlib import ExitStack
import concourse.bass as bass
import concourse.mybir as mybir
from concourse.bass_utils import run_bass_kernel_spmd

F32 = mybir.dt.float32
BF16 = mybir.dt.bfloat16
AF = mybir.ActivationFunctionType
ALU = mybir.AluOpType
AX = mybir.AxisListType

D = 2048
T = 2048
TM = 1024
HID = 5461
HIDP = 5504
HC = 43
KC = 16
NEG = -1.0e9
EPS = 1e-5

ENGS = ("tensor", "vector", "scalar", "sync", "gpsimd")


def build_nc(debug=False):
    nc = bass.Bass("TRN2", target_bir_lowering=False, detect_race_conditions=False)

    x_loc = nc.dram_tensor("x_loc", [D, T], F32, kind="ExternalInput").ap()
    masks_d = nc.dram_tensor("masks", [2, 128, 128], BF16, kind="ExternalInput").ap()
    ident_d = nc.dram_tensor("ident", [128, 128], BF16, kind="ExternalInput").ap()
    wp_d = nc.dram_tensor("wp", [48, 128, 2048], F32, kind="ExternalInput").ap()
    wout_d = nc.dram_tensor("wout", [16, 128, 2048], F32, kind="ExternalInput").ap()
    w1_d = nc.dram_tensor("w1t", [HC, 128, 2048], F32, kind="ExternalInput").ap()
    w3_d = nc.dram_tensor("w3t", [HC, 128, 2048], F32, kind="ExternalInput").ap()
    w2q_d = nc.dram_tensor("w2q", [64, 128, 1408], F32, kind="ExternalInput").ap()
    out_d = nc.dram_tensor("out", [D, TM], F32, kind="ExternalOutput").ap()
    sbounce = nc.dram_tensor("sbounce", [128, 3], F32).ap()

    dbg = {}
    if debug:
        for nm, shape, dt in [("dbg_y", [D, T], BF16), ("dbg_O", [D, TM], BF16),
                              ("dbg_z", [D, TM], BF16), ("dbg_x2", [D, TM], BF16),
                              ("dbg_s", [128, 6], F32), ("dbg_nb", [128, 16], F32),
                              ("dbg_g", [HIDP, TM], BF16),
                              ("dbg_z2", [D, TM], BF16),
                              ("dbg_sacc", [128, 3], F32), ("dbg_sbf", [128, 16], BF16),
                              ("dbg_q", [512, TM], BF16), ("dbg_k", [512, T], BF16),
                              ("dbg_v", [T, 512], BF16),
                              ("dbg_R1", [128, TM], BF16), ("dbg_R2", [128, TM], BF16),
                              ("dbg_y2", [D, TM], BF16),
                              ("dbg_ra", [1, TM], F32), ("dbg_rb", [1, TM], F32)]:
            dbg[nm] = nc.dram_tensor(nm, shape, dt, kind="ExternalOutput").ap()

    ctx = ExitStack()
    ctx_ab = ExitStack()
    ctx_c1 = ExitStack()
    ctx_c2 = ExitStack()
    ctx_d = ExitStack()

    def sb(st, n, s, dt):
        return st.enter_context(nc.sbuf_tensor(n, s, dt))

    onesr = sb(ctx, "onesr", [1, 128], BF16)
    onesc = sb(ctx, "onesc", [128, 1], BF16)
    consts = sb(ctx, "consts", [1, 8], F32)
    sth = sb(ctx, "sth", [128, 6], F32)
    sclc = sb(ctx, "sclc", [128, 2], F32)
    cinv = sb(ctx, "cinv", [128, 1], F32)
    negone = sb(ctx, "negone", [128, 1], F32)
    c128i = sb(ctx, "c128i", [128, 1], F32)
    chalf = sb(ctx, "chalf", [128, 1], F32)
    btmp = sb(ctx, "btmp", [128, 2], F32)
    drn = sb(ctx, "drn", [1, 4], F32)
    sacc = sb(ctx, "sacc", [128, 3], F32)
    sacct = sb(ctx, "sacct", [128, 2], F32)
    sbf = sb(ctx, "sbf", [128, 8], BF16)
    sbflo = sb(ctx, "sbflo", [128, 8], BF16)
    sfull = sb(ctx, "sfull", [128, 3], F32)
    ones128 = sb(ctx, "ones128", [128, 128], BF16)
    bigrow = sb(ctx, "bigrow", [1, 512], BF16)
    cs2 = sb(ctx, "cs2", [128, 1], F32)
    csn = sb(ctx, "csn", [128, 1], F32)
    rowbf = sb(ctx, "rowbf", [1, 1024], BF16)
    vbf = None
    rowa = sb(ctx, "rowa", [1, 1024], F32)
    rowb = sb(ctx, "rowb", [1, 1024], F32)
    REP1 = sb(ctx, "REP1", [128, 1024], BF16)
    REP2 = sb(ctx, "REP2", [128, 1024], BF16)
    sq = [sb(ctx, f"sq{i}", [128, 2048], BF16) for i in range(2)]
    strm = [sb(ctx, f"strm{i}", [128, 2048], F32) for i in range(2)]
    ctx_o = ExitStack()
    O = [sb(ctx_o, f"Ot{i}", [128, TM], BF16) for i in range(16)]
    ident = sb(ctx_ab, "identt", [128, 128], BF16)
    negB = sb(ctx_ab, "negB", [128, 16], F32)
    nrm = sb(ctx_ab, "nrm", [1, 8], F32)

    ptil = [sb(ctx_ab, f"pt{i}", [128, 1024], BF16) for i in range(2)]
    maskt = sb(ctx_ab, "maskt", [128, 256], BF16)
    y = [sb(ctx_ab, f"y{k}", [128, T], BF16) for k in range(KC)]
    qq = [sb(ctx_ab, f"q{i}", [128, TM], BF16) for i in range(4)]
    kq = [sb(ctx_ab, f"k{i}", [128, T], BF16) for i in range(4)]
    vq = [sb(ctx_ab, f"v{i}", [128, 512], BF16) for i in range(16)]
    wbuf = [sb(ctx_ab, f"wb{i}", [128, 2048], BF16) for i in range(5)]

    P0 = ctx.enter_context(nc.psum_tensor("P0", [128, 1024], F32))
    P1 = ctx.enter_context(nc.psum_tensor("P1", [128, 1024], F32))
    P2 = ctx.enter_context(nc.psum_tensor("P2", [128, 512], F32))
    P3 = ctx.enter_context(nc.psum_tensor("P3", [128, 512], F32))
    P4 = ctx.enter_context(nc.psum_tensor("P4", [128, 512], F32))
    P5 = ctx.enter_context(nc.psum_tensor("P5", [128, 512], F32))

    sems = {}
    for s in ("ten", "vec", "sca", "gpm", "dgc", "dgx", "dgb", "dxm", "out", "stm",
              "wps0", "wps1", "wps2", "wps3", "wps4", "sts0", "sts1", "stg1", "sto0", "sto1"):
        sems[s] = ctx.enter_context(nc.semaphore(s))
    cnt = {s: 0 for s in sems}
    streams = {e: [] for e in ENGS}
    SEM = sems

    def em(e, f):
        streams[e].append(f)

    def wt(e, s, v):
        if v and v > 0:
            h = sems[s]
            streams[e].append(lambda eng, h=h, v=v: eng.wait_ge(h, v))

    def bump(s, n=1):
        cnt[s] += n
        return cnt[s]

    pfree = {}

    def use_psum(e, *names):
        for n in names:
            d = pfree.get(n)
            if d:
                wt(e, d[0], d[1])

    def set_pfree(names, s, v):
        for n in names:
            pfree[n] = (s, v)

    def _init(g):
        g.memset(onesr[:], 1.0)
        g.memset(onesc[:], 1.0)
        g.memset(consts[0:1, 0:1], 1.0 / D)
        g.memset(consts[0:1, 1:2], EPS)
        g.memset(consts[0:1, 2:3], -0.5)
        g.memset(consts[0:1, 3:4], 1.0 / 128.0)
        g.memset(consts[0:1, 4:5], 0.5 / (HID * D))
        g.memset(consts[0:1, 5:6], 0.5 / (HID * D))
        g.memset(consts[0:1, 6:7], 0.5)
        g.memset(sacc[:], 0.0)
        g.memset(sbf[:], 0.0)
        g.memset(sbflo[:], 0.0)
        g.memset(ones128[:], 1.0)
        g.memset(cs2[:], 0.5 / (HID * D))
        g.memset(csn[:], -0.5 / (HID * D))
        g.memset(negone[:], -1.0)
        g.memset(c128i[:], 1.0 / 128.0)
        g.memset(chalf[:], 0.5)
        g.memset(cinv[:], 1.0 / (128.0 ** 0.5)).then_inc(SEM["gpm"], 1)
    em("gpsimd", _init)
    gpm_init = bump("gpm")
    em("gpsimd", lambda g: g.dma_start(out=ident[:], in_=ident_d).then_inc(SEM["dgc"], 16))
    em("gpsimd", lambda g: g.dma_start(out=maskt[:, 0:128], in_=masks_d[0]).then_inc(SEM["dgc"], 16))
    em("gpsimd", lambda g: g.dma_start(out=maskt[:, 128:256], in_=masks_d[1]).then_inc(SEM["dgc"], 16))
    bump("dgc", 48)
    for k in range(KC):
        em("gpsimd", lambda g, k=k: g.dma_start(
            out=y[k][:], in_=x_loc[128 * k:128 * (k + 1), :]).then_inc(SEM["dgx"], 16))
        bump("dgx", 16)

    ten_st = 0
    for k in range(KC):
        wt("scalar", "dgx", 256)
        wt("scalar", "gpm", gpm_init)
        if k >= 2:
            wt("scalar", "ten", ten_st - 1)
        em("scalar", lambda s, k=k: s.activation(
            sq[k % 2][:], y[k][:], AF.Square).then_inc(SEM["sca"], 1))
        schk = bump("sca")
        wt("tensor", "sca", schk)
        if k == 0:
            use_psum("tensor", "P0", "P1", "P2", "P3", "P4", "P5")
        def _stat(t, k=k):
            for gr in range(4):
                pr = (P0 if gr < 2 else P1)[0:1, 512 * (gr % 2):512 * (gr % 2) + 512]
                t.matmul(pr, onesc[:], y[k][:, 512 * gr:512 * (gr + 1)],
                         start=(k == 0), stop=(k == KC - 1))
            ins = None
            for gr in range(4):
                pr = [P2, P3, P4, P5][gr][0:1, 0:512]
                ins = t.matmul(pr, onesc[:], sq[k % 2][:, 512 * gr:512 * (gr + 1)],
                               start=(k == 0), stop=(k == KC - 1))
            ins.then_inc(SEM["ten"], 1)
        em("tensor", _stat)
        ten_st = bump("ten")

    def ln_rows(Tn, psx_rows, psq_rows, ten_dep, rowa, rowb, REP1, REP2):
        ng = Tn // 512
        wt("vector", "ten", ten_dep)
        def _rows(v):
            for gr in range(ng):
                v.tensor_single_scalar(rowa[0:1, 512 * gr:512 * (gr + 1)], psx_rows[gr],
                                       consts[0:1, 0:1], ALU.mult)
            v.tensor_mul(rowb[0:1, 0:Tn], rowa[0:1, 0:Tn], rowa[0:1, 0:Tn])
            for gr in range(ng):
                v.scalar_tensor_tensor(rowb[0:1, 512 * gr:512 * (gr + 1)], psq_rows[gr],
                                       consts[0:1, 0:1], rowb[0:1, 512 * gr:512 * (gr + 1)],
                                       ALU.mult, ALU.subtract)
            v.tensor_copy(rowbf[0:1, 0:Tn], rowb[0:1, 0:Tn])
            v.tensor_copy(drn[0:1, 0:2], rowbf[0:1, Tn - 2:Tn]).then_inc(SEM["vec"], 1)
        em("vector", _rows)
        vchk = bump("vec")
        rows_chk = vchk
        wt("tensor", "vec", vchk)
        use_psum("tensor", "P2", "P3")
        def _vmm(t):
            t.matmul(P2[0:1, 0:512], onesc[0:1, 0:1], rowbf[0:1, 0:512], start=True, stop=True)
            t.matmul(P3[0:1, 0:512], onesc[0:1, 0:1], rowbf[0:1, 512:1024],
                     start=True, stop=True).then_inc(SEM["ten"], 1)
        em("tensor", _vmm)
        tvchk = bump("ten")
        wt("scalar", "ten", tvchk)
        def _rstd(s):
            s.activation(rowb[0:1, 0:512], P2[0:1, 0:512], AF.Ln, bias=consts[0:1, 1:2])
            s.activation(rowb[0:1, 512:1024], P3[0:1, 0:512], AF.Ln, bias=consts[0:1, 1:2])
            s.activation(rowb[0:1, 0:Tn], rowb[0:1, 0:Tn], AF.Exp, scale=consts[0:1, 2:3])
            s.activation(drn[0:1, 2:3], rowb[0:1, 0:1], AF.Copy).then_inc(SEM["sca"], 1)
        em("scalar", _rstd)
        schk = bump("sca")
        set_pfree(["P2", "P3"], "sca", schk)
        wt("vector", "sca", schk)
        def _casts(v):
            v.tensor_copy(rowbf[0:1, 0:Tn], rowb[0:1, 0:Tn])
            v.tensor_mul(sq[1][0:1, 0:Tn], rowa[0:1, 0:Tn], rowb[0:1, 0:Tn])
            v.tensor_copy(drn[0:1, 0:2], sq[1][0:1, Tn - 2:Tn]).then_inc(SEM["vec"], 1)
        em("vector", _casts)
        vchk2 = bump("vec")
        last_v = 0
        for gr in range(ng):
            wt("tensor", "vec", vchk2)
            use_psum("tensor", "P2", "P3")
            def _bc(t, gr=gr):
                t.matmul(P2[:, 0:512], onesr[:], rowbf[0:1, 512 * gr:512 * (gr + 1)],
                         start=True, stop=True)
                t.matmul(P3[:, 0:512], onesr[:], sq[1][0:1, 512 * gr:512 * (gr + 1)],
                         start=True, stop=True).then_inc(SEM["ten"], 1)
            em("tensor", _bc)
            tchk = bump("ten")
            wt("vector", "ten", tchk)
            def _ev(v, gr=gr):
                v.tensor_copy(REP1[:, 512 * gr:512 * (gr + 1)], P2[:])
                v.tensor_copy(REP2[:, 512 * gr:512 * (gr + 1)], P3[:]).then_inc(SEM["vec"], 1)
            em("vector", _ev)
            last_v = bump("vec")
            set_pfree(["P2", "P3"], "vec", last_v)
        return rows_chk, last_v

    set_pfree(["P0", "P1", "P2", "P3", "P4", "P5"], None, 0)
    for half in range(2):
        psx = [P0[0:1, 0:512], P0[0:1, 512:1024]] if half == 0 else \
              [P1[0:1, 0:512], P1[0:1, 512:1024]]
        psq = [P2[0:1, 0:512], P3[0:1, 0:512]] if half == 0 else \
              [P4[0:1, 0:512], P5[0:1, 0:512]]
        rows1, vec_rep1 = ln_rows(TM, psx, psq, ten_st, rowa, rowb, REP1, REP2)
        if half == 1:
            set_pfree(["P0", "P1", "P4", "P5"], "vec", rows1)
        for k in range(KC):
            wt("vector", "vec", vec_rep1)
            def _ap(v, k=k, half=half):
                c0, c1 = 1024 * half, 1024 * (half + 1)
                v.tensor_mul(y[k][:, c0:c1], y[k][:, c0:c1], REP1[:, 0:TM])
                v.tensor_sub(y[k][:, c0:c1], y[k][:, c0:c1],
                             REP2[:, 0:TM]).then_inc(SEM["vec"], 1)
            em("vector", _ap)
            bump("vec")
    vec_ydone = cnt["vec"]

    if debug:
        for k in range(KC):
            wt("sync", "vec", vec_ydone)
            em("sync", lambda s, k=k: s.dma_start(
                out=dbg["dbg_y"][128 * k:128 * (k + 1), :], in_=y[k][:]).then_inc(SEM["out"], 16))
            bump("out", 16)
        em("sync", lambda s: s.dma_start(out=dbg["dbg_R1"], in_=REP1[:, 0:TM]).then_inc(SEM["out"], 16))
        bump("out", 16)
        em("sync", lambda s: s.dma_start(out=dbg["dbg_R2"], in_=REP2[:, 0:TM]).then_inc(SEM["out"], 16))
        bump("out", 16)
        em("sync", lambda s: s.dma_start(out=dbg["dbg_ra"], in_=rowa[0:1, 0:TM]).then_inc(SEM["out"], 16))
        bump("out", 16)
        em("sync", lambda s: s.dma_start(out=dbg["dbg_rb"], in_=rowb[0:1, 0:TM]).then_inc(SEM["out"], 16))
        bump("out", 16)
        wt("gpsimd", "out", cnt["out"])
        em("gpsimd", lambda g: g.memset(consts[0:1, 7:8], 0.0).then_inc(SEM["stm"], 1))
        bump("stm")

    scan_list = []
    for i in range(HC):
        scan_list.append((w1_d, i, 2048, 0))
    for i in range(HC):
        scan_list.append((w3_d, i, 2048, 1))
    for i in range(64):
        scan_list.append((w2q_d, i, 1408, 2))
    n_scan = len(scan_list)
    sstate = {"ld": 0, "rd": 0, "dsy": [], "vchk": []}

    def scan_load(n):
        for _ in range(n):
            i = sstate["ld"]
            if i >= n_scan:
                return
            dram, idx, w, col = scan_list[i]
            if i >= 2:
                wt("sync", "vec", sstate["vchk"][i - 2])
            sl = "sts%d" % (i % 2)
            em("sync", lambda s, dram=dram, idx=idx, w=w, i=i, sl=sl: s.dma_start(
                out=strm[i % 2][:, 0:w], in_=dram[idx]).then_inc(SEM[sl], 16))
            sstate["dsy"].append(bump(sl, 16))
            sstate["ld"] += 1

    def scan_red(n):
        for _ in range(n):
            i = sstate["rd"]
            if i >= n_scan or i >= sstate["ld"]:
                return
            dram, idx, w, col = scan_list[i]
            wt("vector", "sts%d" % (i % 2), sstate["dsy"][i])
            pcol = scan_list[i - 1][3] if i >= 1 else None
            def _r(v, w=w, i=i, pcol=pcol):
                v.reduce_sum(sacct[:, i % 2:i % 2 + 1], strm[i % 2][:, 0:w], axis=AX.X,
                             apply_absolute_value=True)
                ins = v.tensor_add(sacc[:, pcol:pcol + 1], sacc[:, pcol:pcol + 1],
                                   sacct[:, (i - 1) % 2:(i - 1) % 2 + 1]) if pcol is not None else \
                    v.tensor_copy(drn[0:1, 0:2], consts[0:1, 0:2])
                ins.then_inc(SEM["vec"], 1)
            em("vector", _r)
            sstate["vchk"].append(bump("vec"))
            sstate["rd"] += 1

    def scan_flush():
        i = n_scan - 1
        col = scan_list[i][3]
        def _rf(v, i=i, col=col):
            v.tensor_copy(ptil[0][:, 0:512], ptil[0][:, 0:512])
            v.tensor_copy(ptil[0][:, 512:1024], ptil[0][:, 512:1024])
            v.tensor_add(sacc[:, col:col + 1], sacc[:, col:col + 1],
                         sacct[:, i % 2:i % 2 + 1]).then_inc(SEM["vec"], 1)
        em("vector", _rf)
        return bump("vec")

    wp_dgp = {}
    q_done = {}
    wp_state = {"next": 0}

    def wp_fill(limit):
        while wp_state["next"] < min(48, limit):
            i = wp_state["next"]
            if i >= 5:
                wt("gpsimd", "ten", q_done[i - 5])
            sl = "wps%d" % (i % 5)
            em("gpsimd", lambda g, i=i, sl=sl: g.dma_start(
                out=wbuf[i % 5][:], in_=wp_d[i]).then_inc(SEM[sl], 16))
            wp_dgp[i] = (sl, bump(sl, 16))
            wp_state["next"] += 1

    wp_fill(5)

    sca_exp = {}
    ten_pv = {}
    vec_fin = {}

    if debug:
        wt("vector", "stm", cnt["stm"])
    for Q in range(4):
        bq = 12 * Q
        wt("tensor", wp_dgp[bq + 3][0], wp_dgp[bq + 3][1])
        wt("tensor", wp_dgp[bq + 0][0], wp_dgp[bq + 0][1])
        wt("tensor", wp_dgp[bq + 1][0], wp_dgp[bq + 1][1])
        wt("tensor", wp_dgp[bq + 2][0], wp_dgp[bq + 2][1])
        wt("tensor", "dgc", 48)
        wt("tensor", "vec", vec_ydone)
        for hh in range(4):
            for gr in range(2):
                pn = ["P2", "P3"][gr]
                ps = [P2, P3][gr]
                use_psum("tensor", pn)
                def _qmm(t, hh=hh, gr=gr, bq=bq, ps=ps):
                    ins = None
                    for kd in range(KC):
                        ins = t.matmul(ps[:, 0:512],
                                       wbuf[(bq + kd // 4) % 5][:, 512 * (kd % 4) + 128 * hh:512 * (kd % 4) + 128 * hh + 128],
                                       y[kd][:, 512 * gr:512 * (gr + 1)],
                                       start=(kd == 0), stop=(kd == KC - 1))
                    ins.then_inc(SEM["ten"], 1)
                em("tensor", _qmm)
                tchk = bump("ten")
                wt("vector", "ten", tchk)
                em("vector", lambda v, hh=hh, gr=gr, ps=ps: v.tensor_copy(
                    qq[hh][:, 512 * gr:512 * (gr + 1)], ps[:]).then_inc(SEM["vec"], 1))
                set_pfree([pn], "vec", bump("vec"))
        for j in range(4):
            q_done[bq + j] = cnt["ten"]
        wp_fill(bq + 9)
        for j in range(4, 8):
            wt("tensor", wp_dgp[bq + j][0], wp_dgp[bq + j][1])
        for hh in range(4):
            for gr in range(4):
                pn = ["P2", "P3", "P4", "P5"][gr]
                ps = [P2, P3, P4, P5][gr]
                use_psum("tensor", pn)
                def _kmm(t, hh=hh, gr=gr, bq=bq, ps=ps):
                    ins = None
                    for kd in range(KC):
                        ins = t.matmul(ps[:, 0:512],
                                       wbuf[(bq + 4 + kd // 4) % 5][:, 512 * (kd % 4) + 128 * hh:512 * (kd % 4) + 128 * hh + 128],
                                       y[kd][:, 512 * gr:512 * (gr + 1)],
                                       start=(kd == 0), stop=(kd == KC - 1))
                    ins.then_inc(SEM["ten"], 1)
                em("tensor", _kmm)
                tchk = bump("ten")
                wt("vector", "ten", tchk)
                em("vector", lambda v, hh=hh, gr=gr, ps=ps: v.tensor_copy(
                    kq[hh][:, 512 * gr:512 * (gr + 1)], ps[:]).then_inc(SEM["vec"], 1))
                set_pfree([pn], "vec", bump("vec"))
        for j in range(4, 8):
            q_done[bq + j] = cnt["ten"]
        wp_fill(bq + 13)
        for j in range(8, 12):
            wt("tensor", wp_dgp[bq + j][0], wp_dgp[bq + j][1])
        for tc in range(16):
            pn = ["P2", "P3"][tc % 2]
            ps = [P2, P3][tc % 2]
            use_psum("tensor", pn)
            def _vmm(t, tc=tc, bq=bq, ps=ps):
                ins = None
                for kd in range(KC):
                    ins = t.matmul(ps[:, 0:512],
                                   y[kd][:, 128 * tc:128 * (tc + 1)],
                                   wbuf[(bq + 8 + kd // 4) % 5][:, 512 * (kd % 4):512 * (kd % 4) + 512],
                                   start=(kd == 0), stop=(kd == KC - 1))
                ins.then_inc(SEM["ten"], 1)
            em("tensor", _vmm)
            tchk = bump("ten")
            wt("vector", "ten", tchk)
            em("vector", lambda v, tc=tc, ps=ps: v.tensor_copy(
                vq[tc][:], ps[:]).then_inc(SEM["vec"], 1))
            set_pfree([pn], "vec", bump("vec"))
        for j in range(8, 12):
            q_done[bq + j] = cnt["ten"]
        wp_fill(bq + 17)
        for _ in range(15):
            scan_load(1)
            scan_red(1)

        for hh in range(4):
            h = 4 * Q + hh
            wt("scalar", "vec", cnt["vec"])
            def _sqr(s, hh=hh):
                s.activation(sq[0][:, 0:1024], qq[hh][:], AF.Square)
                s.activation(sq[1][:], kq[hh][:], AF.Square).then_inc(SEM["sca"], 1)
            em("scalar", _sqr)
            schk = bump("sca")
            wt("tensor", "sca", schk)
            use_psum("tensor", "P2", "P3", "P4", "P5", "P0")
            def _nmm(t):
                t.matmul(P4[0:1, 0:512], onesc[:], sq[0][:, 0:512], start=True, stop=True)
                t.matmul(P5[0:1, 0:512], onesc[:], sq[0][:, 512:1024], start=True, stop=True)
                t.matmul(P2[0:1, 0:512], onesc[:], sq[1][:, 0:512], start=True, stop=True)
                t.matmul(P3[0:1, 0:512], onesc[:], sq[1][:, 512:1024], start=True, stop=True)
                t.matmul(P0[0:1, 0:512], onesc[:], sq[1][:, 1024:1536], start=True, stop=True)
                t.matmul(P0[0:1, 512:1024], onesc[:], sq[1][:, 1536:2048],
                         start=True, stop=True).then_inc(SEM["ten"], 1)
            em("tensor", _nmm)
            tchk = bump("ten")
            wt("vector", "ten", tchk)
            def _bv(v, h=h, hh=hh):
                v.tensor_copy(rowa[0:1, 0:512], P4[0:1, 0:512])
                v.tensor_copy(rowa[0:1, 512:1024], P5[0:1, 0:512])
                v.reduce_max(nrm[0:1, 0:1], rowa[0:1, 0:1024], axis=AX.X)
                v.tensor_copy(rowa[0:1, 0:512], P2[0:1, 0:512])
                v.tensor_copy(rowa[0:1, 512:1024], P3[0:1, 0:512])
                v.reduce_max(nrm[0:1, 4:5], rowa[0:1, 0:1024], axis=AX.X)
                v.tensor_copy(rowa[0:1, 0:512], P0[0:1, 0:512])
                v.tensor_copy(rowa[0:1, 512:1024], P0[0:1, 512:1024])
                v.reduce_max(nrm[0:1, 5:6], rowa[0:1, 0:1024], axis=AX.X)
                v.tensor_copy(drn[0:1, 0:2], rowa[0:1, 0:2])
                v.tensor_max(nrm[0:1, 1:2], nrm[0:1, 4:5], nrm[0:1, 5:6])
                v.tensor_copy(drn[0:1, 2:4], rowa[0:1, 2:4])
                v.tensor_mul(nrm[0:1, 2:3], nrm[0:1, 0:1], nrm[0:1, 1:2])
                v.tensor_copy(drn[0:1, 0:2], rowa[0:1, 4:6])
                for j in range(4):
                    v.tensor_single_scalar(bigrow[0:1, 128 * j:128 * (j + 1)],
                                           onesr[0:1, 0:128], nrm[0:1, 2:3], ALU.mult)
                v.tensor_copy(drn[0:1, 2:4], bigrow[0:1, 508:510]).then_inc(SEM["vec"], 1)
            em("vector", _bv)
            vchk = bump("vec")
            set_pfree(["P2", "P3", "P4", "P5", "P0"], "vec", vchk)
            wt("tensor", "vec", vchk)
            use_psum("tensor", "P4")
            em("tensor", lambda t: t.matmul(P4[:, 0:512], onesr[:], bigrow[0:1, 0:512],
                                            start=True, stop=True).then_inc(SEM["ten"], 1))
            tchk2 = bump("ten")
            wt("scalar", "ten", tchk2)
            def _bs(s, h=h):
                s.activation(btmp[:, 0:1], P4[:, 0:1], AF.Ln, scale=c128i[:, 0:1])
                s.activation(btmp[:, 1:2], btmp[:, 0:1], AF.Exp, scale=chalf[:, 0:1])
                s.activation(negB[:, h:h + 1], btmp[:, 1:2], AF.Copy,
                             scale=negone[:, 0:1]).then_inc(SEM["sca"], 1)
            em("scalar", _bs)
            schk2 = bump("sca")
            set_pfree(["P4"], "sca", schk2)
        vec_negb = cnt["vec"]

        for hh in range(4):
            h = 4 * Q + hh

            def emit_pv(p, hh=hh, h=h):
                wt("tensor", "sca", sca_exp[(h, p)])
                if p == 0:
                    use_psum("tensor", "P2", "P3", "P4", "P5")
                qs = 128 * (p % 8)
                def _pv(t, hh=hh, p=p, qs=qs):
                    if qs < 512:
                        t.matmul(P2[:, qs:512], vq[p][:, 128 * hh:128 * (hh + 1)],
                                 ptil[p % 2][:, qs:512], start=(p == 0), stop=(p == 11))
                        t.matmul(P4[0:1, qs:512], onesc[:], ptil[p % 2][:, qs:512],
                                 start=(p == 0), stop=(p == 11))
                    gs = max(qs, 512)
                    t.matmul(P3[:, gs - 512:512], vq[p][:, 128 * hh:128 * (hh + 1)],
                             ptil[p % 2][:, gs:1024], start=(p == 0), stop=(p == 15))
                    t.matmul(P5[0:1, gs - 512:512], onesc[:], ptil[p % 2][:, gs:1024],
                             start=(p == 0), stop=(p == 15)).then_inc(SEM["ten"], 1)
                em("tensor", _pv)
                ten_pv[(h, p)] = bump("ten")

            for p in range(16):
                qs = 128 * (p % 8)
                sn = ["P0", "P1"][p % 2]
                sT = [P0, P1][p % 2]
                wt("tensor", "vec", vec_negb)
                use_psum("tensor", sn)
                def _qk(t, hh=hh, p=p, qs=qs, sT=sT):
                    if qs < 512:
                        t.matmul(sT[:, qs:512], kq[hh][:, 128 * p:128 * (p + 1)],
                                 qq[hh][:, qs:512], start=True, stop=False)
                        t.matmul(sT[:, 512:1024], kq[hh][:, 128 * p:128 * (p + 1)],
                                 qq[hh][:, 512:1024], start=True, stop=True)
                    else:
                        t.matmul(sT[:, qs:1024], kq[hh][:, 128 * p:128 * (p + 1)],
                                 qq[hh][:, qs:1024], start=True, stop=False)
                    mk = maskt[:, 0:128] if p < 8 else maskt[:, 128:256]
                    t.matmul(sT[:, qs:qs + 128], ident[:], mk,
                             start=False, stop=True).then_inc(SEM["ten"], 1)
                em("tensor", _qk)
                tchk = bump("ten")
                wt("scalar", "ten", tchk)
                if p >= 2:
                    wt("scalar", "ten", ten_pv.get((h, p - 2), 0))
                elif h > 0:
                    wt("scalar", "vec", vec_fin.get(h - 1, 0))
                def _exp(s, h=h, p=p, qs=qs, sT=sT):
                    s.activation(ptil[p % 2][:, qs:1024], sT[:, qs:1024], AF.Exp,
                                 bias=negB[:, h:h + 1],
                                 scale=cinv[:, 0:1]).then_inc(SEM["sca"], 1)
                em("scalar", _exp)
                sca_exp[(h, p)] = bump("sca")
                set_pfree([sn], "sca", sca_exp[(h, p)])
                if p >= 1:
                    emit_pv(p - 1)
            emit_pv(15)
            wt("vector", "ten", ten_pv[(h, 15)])
            def _fv(v):
                with nc.allow_low_precision(reason="softmax denom recip bf16; LN-invariant"):
                    v.reciprocal(rowbf[0:1, 0:512], P4[0:1, 0:512])
                    v.reciprocal(rowbf[0:1, 512:1024], P5[0:1, 0:512])
                v.tensor_copy(drn[0:1, 0:2], rowbf[0:1, 1022:1024]).then_inc(SEM["vec"], 1)
            em("vector", _fv)
            vchk = bump("vec")
            wt("tensor", "vec", vchk)
            use_psum("tensor", "P0")
            def _ft(t):
                t.matmul(P0[:, 0:512], onesr[:], rowbf[0:1, 0:512], start=True, stop=True)
                t.matmul(P0[:, 512:1024], onesr[:], rowbf[0:1, 512:1024],
                         start=True, stop=True).then_inc(SEM["ten"], 1)
            em("tensor", _ft)
            tchk = bump("ten")
            wt("vector", "ten", tchk)
            def _fv2(v, h=h):
                v.tensor_copy(ptil[0][:, 0:512], P0[:, 0:512])
                v.tensor_copy(ptil[0][:, 512:1024], P0[:, 512:1024])
                v.tensor_mul(O[h][:, 0:512], P2[:], ptil[0][:, 0:512])
                v.tensor_mul(O[h][:, 512:1024], P3[:], ptil[0][:, 512:1024]).then_inc(SEM["vec"], 1)
            em("vector", _fv2)
            vec_fin[h] = bump("vec")
            set_pfree(["P0", "P2", "P3", "P4", "P5"], "vec", vec_fin[h])
        for _ in range(14):
            scan_load(1)
            scan_red(1)

    ten_attn_done = cnt["ten"]
    vec_attn_done = cnt["vec"]
    while sstate["rd"] < n_scan:
        scan_load(1)
        scan_red(1)
    scan_flush()
    vec_scan_done = cnt["vec"]

    if debug:
        for h in range(16):
            wt("sync", "vec", vec_attn_done)
            em("sync", lambda s, h=h: s.dma_start(
                out=dbg["dbg_O"][128 * h:128 * (h + 1), :], in_=O[h][:]).then_inc(SEM["out"], 16))
            bump("out", 16)
        wt("sync", "vec", vec_attn_done)
        em("sync", lambda s: s.dma_start(out=dbg["dbg_nb"], in_=negB[:]).then_inc(SEM["out"], 16))
        bump("out", 16)
        for hh in range(4):
            em("sync", lambda s, hh=hh: s.dma_start(
                out=dbg["dbg_q"][128 * hh:128 * (hh + 1), :], in_=qq[hh][:]).then_inc(SEM["out"], 16))
            bump("out", 16)
            em("sync", lambda s, hh=hh: s.dma_start(
                out=dbg["dbg_k"][128 * hh:128 * (hh + 1), :], in_=kq[hh][:]).then_inc(SEM["out"], 16))
            bump("out", 16)
        for tc in range(16):
            em("sync", lambda s, tc=tc: s.dma_start(
                out=dbg["dbg_v"][128 * tc:128 * (tc + 1), :], in_=vq[tc][:]).then_inc(SEM["out"], 16))
            bump("out", 16)
        wt("gpsimd", "out", cnt["out"])
        em("gpsimd", lambda g: g.memset(consts[0:1, 7:8], 0.0).then_inc(SEM["stm"], 1))
        bump("stm")

    def _sdr(v):
        v.tensor_copy(ptil[0][:, 0:512], ptil[0][:, 0:512])
        v.tensor_copy(ptil[0][:, 512:1024], ptil[0][:, 512:1024])
        v.tensor_copy(sacct[:, 0:1], sacc[:, 0:1]).then_inc(SEM["vec"], 1)
    em("vector", _sdr)
    vec_scan_drain = bump("vec")
    wt("gpsimd", "vec", vec_scan_drain)
    em("gpsimd", lambda g: g.dma_start(out=sbounce, in_=sacc[:]).then_inc(SEM["dgb"], 16))
    d1 = bump("dgb", 16)
    wt("gpsimd", "dgb", d1)
    em("gpsimd", lambda g: g.dma_start(out=sbf[:, 0:3], in_=sbounce).then_inc(SEM["dgb"], 16))
    bump("dgb", 16)
    em("gpsimd", lambda g: g.dma_start(out=sfull[:], in_=sbounce).then_inc(SEM["dgb"], 16))
    d2 = bump("dgb", 16)
    wt("gpsimd", "dgb", d2)
    em("gpsimd", lambda g: g.tensor_sub(sbflo[:, 0:3], sfull[:], sbf[:, 0:3]).then_inc(SEM["gpm"], 1))
    gpm_lo = bump("gpm")
    wt("tensor", "gpm", gpm_lo)
    use_psum("tensor", "P4")
    def _smm(t):
        t.matmul(P4[:, 0:8], ones128[:, 0:128], sbf[:, 0:8], start=True, stop=False)
        t.matmul(P4[:, 0:8], ones128[:, 0:128], sbflo[:, 0:8],
                 start=False, stop=True).then_inc(SEM["ten"], 1)
    em("tensor", _smm)
    tchk = bump("ten")
    wt("scalar", "ten", tchk)
    def _ses(s):
        s.activation(sth[:, 0:1], P4[:, 0:1], AF.Copy, scale=cs2[:, 0:1])
        s.activation(sth[:, 1:2], P4[:, 0:1], AF.Copy, scale=csn[:, 0:1])
        s.activation(sth[:, 2:3], P4[:, 1:2], AF.Copy, scale=cs2[:, 0:1])
        s.activation(sth[:, 3:4], P4[:, 1:2], AF.Copy, scale=csn[:, 0:1])
        s.activation(sth[:, 4:5], P4[:, 2:3], AF.Copy, scale=cs2[:, 0:1])
        s.activation(sth[:, 5:6], P4[:, 2:3], AF.Copy, scale=csn[:, 0:1])
        s.activation(sclc[:, 0:1], P4[:, 0:1], AF.Copy, scale=cs2[:, 0:1])
        s.activation(sclc[:, 1:2], P4[:, 1:2], AF.Copy, scale=cs2[:, 0:1]).then_inc(SEM["sca"], 1)
    em("scalar", _ses)
    sca_scales = bump("sca")
    set_pfree(["P4"], "sca", sca_scales)

    if debug:
        wt("sync", "gpm", gpm_lo)
        em("sync", lambda s: s.dma_start(out=dbg["dbg_sacc"], in_=sacc[:]).then_inc(SEM["out"], 16))
        bump("out", 16)
        em("sync", lambda s: s.dma_start(out=dbg["dbg_sbf"][:, 0:8], in_=sbf[:]).then_inc(SEM["out"], 16))
        bump("out", 16)
        em("sync", lambda s: s.dma_start(out=dbg["dbg_sbf"][:, 8:16], in_=sbflo[:]).then_inc(SEM["out"], 16))
        bump("out", 16)

    if debug:
        wt("sync", "sca", sca_scales)
        em("sync", lambda s: s.dma_start(out=dbg["dbg_s"], in_=sth[:, 0:6]).then_inc(SEM["out"], 16))
        bump("out", 16)

    ctx_ab.close()
    zsb = [sb(ctx_c2, f"z{i}", [128, TM], BF16) for i in range(16)]
    tfa = sb(ctx_c2, "tfa", [128, TM], F32)
    tfb = sb(ctx_c2, "tfb", [128, TM], F32)
    rowca, rowcb, REPc1, REPc2 = rowa, rowb, REP1, REP2
    xm = [sb(ctx_c1, f"xm{i}", [128, TM], F32) for i in range(16)]
    wob = [sb(ctx_c1, f"wo{i}", [128, 2048], BF16) for i in range(5)]

    for i in range(16):
        wt("sync", "ten", ten_attn_done if i == 0 else 0)
        em("sync", lambda s, i=i: s.dma_start(
            out=xm[i][:], in_=x_loc[128 * i:128 * (i + 1), 0:TM]).then_inc(SEM["dxm"], 16))
        bump("dxm", 16)
    wo_dgp = {}
    wo_done = {}
    wo_state = {"next": 0}

    def wo_fill(limit):
        while wo_state["next"] < min(16, limit):
            i = wo_state["next"]
            if i == 0:
                wt("gpsimd", "ten", ten_attn_done)
                if debug:
                    wt("gpsimd", "stm", cnt["stm"])
            if i >= 5:
                wt("gpsimd", "ten", wo_done[i - 5])
            sl = "wps%d" % (i % 5)
            em("gpsimd", lambda g, i=i, sl=sl: g.dma_start(
                out=wob[i % 5][:], in_=wout_d[i]).then_inc(SEM[sl], 16))
            wo_dgp[i] = (sl, bump(sl, 16))
            wo_state["next"] += 1

    wo_fill(5)

    sca_z = {}
    ten_z = 0
    for o in range(16):
        u = o // 4
        for j in range(4):
            wt("tensor", wo_dgp[4 * u + j][0], wo_dgp[4 * u + j][1])
        if o == 0:
            wt("tensor", "vec", vec_attn_done)
        for gr in range(2):
            pn = ["P2", "P3"][gr]
            ps = [P2, P3][gr]
            use_psum("tensor", pn)
            def _zmm(t, o=o, gr=gr, u=u, ps=ps):
                ins = None
                for kd in range(KC):
                    ins = t.matmul(ps[:, 0:512],
                                   wob[(4 * u + kd // 4) % 5][:, 512 * (kd % 4) + 128 * (o % 4):512 * (kd % 4) + 128 * (o % 4) + 128],
                                   O[kd][:, 512 * gr:512 * (gr + 1)],
                                   start=(kd == 0), stop=(kd == KC - 1))
                ins.then_inc(SEM["ten"], 1)
            em("tensor", _zmm)
            tchk = bump("ten")
            wt("scalar", "ten", tchk)
            em("scalar", lambda s, o=o, gr=gr, ps=ps: s.activation(
                zsb[o][:, 512 * gr:512 * (gr + 1)], ps[:], AF.Copy).then_inc(SEM["sca"], 1))
            sca_z[(o, gr)] = bump("sca")
            set_pfree([pn], "sca", sca_z[(o, gr)])
        if o % 4 == 3:
            for j in range(4):
                wo_done[4 * u + j] = cnt["ten"]
            wo_fill(4 * (u + 1) + 5)
        wt("scalar", "sca", sca_z[(o, 1)])
        if o >= 2:
            wt("scalar", "ten", ten_z)
        em("scalar", lambda s, o=o: s.activation(
            sq[o % 2][:, 0:TM], zsb[o][:], AF.Square).then_inc(SEM["sca"], 1))
        schk = bump("sca")
        wt("tensor", "sca", schk)
        if o == 0:
            use_psum("tensor", "P4", "P5", "P0")
        def _zst(t, o=o):
            t.matmul(P4[0:1, 0:512], onesc[:], zsb[o][:, 0:512], start=(o == 0), stop=(o == 15))
            t.matmul(P5[0:1, 0:512], onesc[:], zsb[o][:, 512:1024], start=(o == 0), stop=(o == 15))
            t.matmul(P0[0:1, 0:512], onesc[:], sq[o % 2][:, 0:512], start=(o == 0), stop=(o == 15))
            t.matmul(P0[0:1, 512:1024], onesc[:], sq[o % 2][:, 512:1024],
                     start=(o == 0), stop=(o == 15)).then_inc(SEM["ten"], 1)
        em("tensor", _zst)
        ten_z = bump("ten")

    if debug:
        for o in range(16):
            wt("sync", "sca", sca_z[(o, 1)])
            em("sync", lambda s, o=o: s.dma_start(
                out=dbg["dbg_z"][128 * o:128 * (o + 1), :], in_=zsb[o][:]).then_inc(SEM["out"], 16))
            bump("out", 16)
        wt("gpsimd", "out", cnt["out"])
        em("gpsimd", lambda g: g.memset(consts[0:1, 7:8], 0.0).then_inc(SEM["stm"], 1))
        bump("stm")

    rows2, vec_rep2 = ln_rows(TM, [P4[0:1, 0:512], P5[0:1, 0:512]],
                              [P0[0:1, 0:512], P0[0:1, 512:1024]], ten_z,
                              rowca, rowcb, REPc1, REPc2)
    set_pfree(["P0", "P4", "P5"], "vec", rows2)

    vec_x2 = []
    for i in range(16):
        wt("vector", "vec", vec_rep2)
        wt("vector", "dxm", 256)
        if debug:
            wt("vector", "stm", cnt["stm"])
        def _x2(v, i=i):
            v.tensor_mul(tfa[:], zsb[i][:], REPc1[:, 0:TM])
            v.tensor_sub(tfa[:], tfa[:], REPc2[:, 0:TM])
            v.tensor_add(zsb[i][:], xm[i][:], tfa[:]).then_inc(SEM["vec"], 1)
        em("vector", _x2)
        vec_x2.append(bump("vec"))
    x2 = zsb
    vec_x2_done = cnt["vec"]

    if debug:
        for i in range(16):
            wt("sync", "vec", vec_x2_done)
            em("sync", lambda s, i=i: s.dma_start(
                out=dbg["dbg_x2"][128 * i:128 * (i + 1), :], in_=x2[i][:]).then_inc(SEM["out"], 16))
            bump("out", 16)
        wt("gpsimd", "out", cnt["out"])
        em("gpsimd", lambda g: g.memset(consts[0:1, 7:8], 0.0).then_inc(SEM["stm"], 1))
        bump("stm")

    ten3 = 0
    for i in range(16):
        if i >= 2:
            wt("vector", "ten", ten3)
        em("vector", lambda v, i=i: v.tensor_mul(
            sq[i % 2][:, 0:TM], x2[i][:], x2[i][:]).then_inc(SEM["vec"], 1))
        vchk = bump("vec")
        wt("tensor", "vec", vchk)
        if i == 0:
            use_psum("tensor", "P4", "P5", "P0")
        def _st3(t, i=i):
            t.matmul(P4[0:1, 0:512], onesc[:], x2[i][:, 0:512], start=(i == 0), stop=(i == 15))
            t.matmul(P5[0:1, 0:512], onesc[:], x2[i][:, 512:1024], start=(i == 0), stop=(i == 15))
            t.matmul(P0[0:1, 0:512], onesc[:], sq[i % 2][:, 0:512], start=(i == 0), stop=(i == 15))
            t.matmul(P0[0:1, 512:1024], onesc[:], sq[i % 2][:, 512:1024],
                     start=(i == 0), stop=(i == 15)).then_inc(SEM["ten"], 1)
        em("tensor", _st3)
        ten3 = bump("ten")
    rows3, vec_rep3 = ln_rows(TM, [P4[0:1, 0:512], P5[0:1, 0:512]],
                              [P0[0:1, 0:512], P0[0:1, 512:1024]], ten3,
                              rowca, rowcb, REPc1, REPc2)
    set_pfree(["P0", "P4", "P5"], "vec", rows3)
    ten_ln3_done = cnt["ten"]

    ctx_c1.close()
    g_t = [sb(ctx_d, f"g{i}", [128, TM], BF16) for i in range(HC)]
    silub = sb(ctx_d, "silub", [128, 512], BF16)
    ctx_d2 = ExitStack()
    tq = [sb(ctx_d2, f"tq{i}", [128, 2048], BF16) for i in range(2)]
    y2 = O

    vec_y2 = []
    for i in range(16):
        wt("vector", "vec", vec_rep3)
        if debug:
            wt("vector", "stm", cnt["stm"])
        def _y2(v, i=i):
            v.tensor_mul(y2[i][:], x2[i][:], REPc1[:, 0:TM])
            v.tensor_sub(y2[i][:], y2[i][:], REPc2[:, 0:TM]).then_inc(SEM["vec"], 1)
        em("vector", _y2)
        vec_y2.append(bump("vec"))
    vec_y2_done = cnt["vec"]

    if debug:
        for i in range(16):
            wt("sync", "vec", vec_y2_done)
            em("sync", lambda s, i=i: s.dma_start(
                out=dbg["dbg_y2"][128 * i:128 * (i + 1), :], in_=y2[i][:]).then_inc(SEM["out"], 16))
            bump("out", 16)
        wt("gpsimd", "out", cnt["out"])
        em("gpsimd", lambda g: g.memset(consts[0:1, 7:8], 0.0).then_inc(SEM["stm"], 1))
        bump("stm")
        wt("scalar", "stm", cnt["stm"])

    dsy_w = {}
    sca_q = {}
    vec_q = {}
    ten_h = {}
    sca_sil = {}
    vec_g = {}

    def w13_load(idx):
        if idx >= 2 * HC:
            return
        hb, wi = idx // 2, idx % 2
        dram = w1_d if wi == 0 else w3_d
        qn = "sync" if idx % 2 == 0 else "gpsimd"
        if idx >= 2:
            wt(qn, "sca", sca_q.get(idx - 2, 0))
        if idx <= 1:
            wt(qn, "vec", vec_scan_drain)
        sl = "sts0" if idx % 2 == 0 else "stg1"
        em(qn, lambda s, dram=dram, hb=hb, idx=idx, sl=sl: s.dma_start(
            out=strm[idx % 2][:, 0:2048], in_=dram[hb]).then_inc(SEM[sl], 16))
        dsy_w[idx] = (sl, bump(sl, 16))

    w13_load(0)
    w13_load(1)
    for hb in range(HC):
        for wi in range(2):
            idx = 2 * hb + wi
            coff = 1 if wi == 0 else 3
            wt("scalar", dsy_w[idx][0], dsy_w[idx][1])
            if idx <= 1:
                wt("scalar", "ten", ten_ln3_done)
            if idx >= 2:
                wt("scalar", "ten", ten_h.get(idx - 2, 0))
                wt("scalar", "vec", vec_q.get(idx - 2, 0))
            def _qnt(s, idx=idx, coff=coff):
                s.activation(tq[idx % 2][:, 0:2048], strm[idx % 2][:, 0:2048], AF.Sign,
                             bias=sth[:, coff:coff + 1])
                s.activation(sq[idx % 2][:, 0:2048], strm[idx % 2][:, 0:2048], AF.Sign,
                             bias=sth[:, coff - 1:coff]).then_inc(SEM["sca"], 1)
            em("scalar", _qnt)
            sca_q[idx] = bump("sca")
            wt("vector", "sca", sca_q[idx])
            em("vector", lambda v, idx=idx: v.tensor_add(
                tq[idx % 2][:, 0:2048], tq[idx % 2][:, 0:2048],
                sq[idx % 2][:, 0:2048]).then_inc(SEM["vec"], 1))
            vec_q[idx] = bump("vec")
            w13_load(idx + 2)
            wt("tensor", "vec", vec_q[idx])
            if idx == 0:
                wt("tensor", "vec", vec_y2_done)
            for gr in range(2):
                pn = (["P2", "P3"] if wi == 0 else ["P4", "P5"])[gr]
                ps = ([P2, P3] if wi == 0 else [P4, P5])[gr]
                use_psum("tensor", pn)
                def _hmm(t, idx=idx, gr=gr, ps=ps):
                    ins = None
                    for kd in range(KC):
                        ins = t.matmul(ps[:, 0:512],
                                       tq[idx % 2][:, 128 * kd:128 * (kd + 1)],
                                       y2[kd][:, 512 * gr:512 * (gr + 1)],
                                       start=(kd == 0), stop=(kd == KC - 1))
                    ins.then_inc(SEM["ten"], 1)
                em("tensor", _hmm)
                bump("ten")
            ten_h[idx] = cnt["ten"]
        for gr in range(2):
            wt("scalar", "ten", ten_h[2 * hb + 1])
            if hb > 0 or gr > 0:
                wt("scalar", "vec", vec_g.get((hb, gr - 1) if gr > 0 else (hb - 1, 1), 0))
            pn1 = ["P2", "P3"][gr]
            pn3 = ["P4", "P5"][gr]
            def _sil(s, gr=gr):
                s.activation(silub[:], [P2, P3][gr][:], AF.Silu,
                             scale=sclc[:, 0:1]).then_inc(SEM["sca"], 1)
            em("scalar", _sil)
            sca_sil[(hb, gr)] = bump("sca")
            set_pfree([pn1], "sca", sca_sil[(hb, gr)])
            wt("vector", "sca", sca_sil[(hb, gr)])
            def _gm(v, hb=hb, gr=gr):
                v.tensor_mul(g_t[hb][:, 512 * gr:512 * (gr + 1)], silub[:],
                             [P4, P5][gr][:]).then_inc(SEM["vec"], 1)
            em("vector", _gm)
            vec_g[(hb, gr)] = bump("vec")
            set_pfree([pn3], "vec", vec_g[(hb, gr)])
    vec_g_done = cnt["vec"]
    ctx_d2.close()
    tq2 = [sb(ctx_d, f"tq2_{i}", [128, 1408], BF16) for i in range(2)]
    q2s = sb(ctx_d, "q2s", [128, 1408], BF16)
    z2 = O

    if debug:
        for hb in range(HC):
            wt("sync", "vec", vec_g_done)
            em("sync", lambda s, hb=hb: s.dma_start(
                out=dbg["dbg_g"][128 * hb:128 * (hb + 1), :], in_=g_t[hb][:]).then_inc(SEM["out"], 16))
            bump("out", 16)

    dsy_w2 = {}
    sca_q2 = {}
    vec_q2 = {}
    ten_w2 = {}
    sca_z2 = {}
    ten_z4 = 0
    NHCQ = (11, 11, 11, 10)

    def w2_load(idx):
        if idx >= 64:
            return
        qn = "sync" if idx % 2 == 0 else "gpsimd"
        if idx >= 2:
            wt(qn, "sca", sca_q2.get(idx - 2, 0))
        else:
            wt(qn, "sca", sca_q[2 * HC - 2 + idx])
        wdt = 128 * NHCQ[idx % 4]
        sl = "sts0" if idx % 2 == 0 else "stg1"
        em(qn, lambda s, idx=idx, wdt=wdt, sl=sl: s.dma_start(
            out=strm[idx % 2][:, 0:wdt], in_=w2q_d[idx][:, 0:wdt]).then_inc(SEM[sl], 16))
        dsy_w2[idx] = (sl, bump(sl, 16))

    w2_load(0)
    w2_load(1)
    for ob in range(16):
        for qt in range(4):
            idx = 4 * ob + qt
            wdt = 128 * NHCQ[qt]
            wt("scalar", dsy_w2[idx][0], dsy_w2[idx][1])
            if idx >= 2:
                wt("scalar", "ten", ten_w2.get(idx - 2, 0))
            if idx >= 1:
                wt("scalar", "vec", vec_q2.get(idx - 1, 0))
            def _q2(s, idx=idx, wdt=wdt):
                s.activation(tq2[idx % 2][:, 0:wdt], strm[idx % 2][:, 0:wdt], AF.Sign,
                             bias=sth[:, 5:6])
                s.activation(q2s[:, 0:wdt], strm[idx % 2][:, 0:wdt], AF.Sign,
                             bias=sth[:, 4:5]).then_inc(SEM["sca"], 1)
            em("scalar", _q2)
            sca_q2[idx] = bump("sca")
            wt("vector", "sca", sca_q2[idx])
            em("vector", lambda v, idx=idx, wdt=wdt: v.tensor_add(
                tq2[idx % 2][:, 0:wdt], tq2[idx % 2][:, 0:wdt],
                q2s[:, 0:wdt]).then_inc(SEM["vec"], 1))
            vec_q2[idx] = bump("vec")
            w2_load(idx + 2)
            wt("tensor", "vec", vec_q2[idx])
            if idx == 0:
                wt("tensor", "vec", vec_g_done)
            hc0 = sum(NHCQ[:qt])
            for gr in range(2):
                pn = ["P2", "P3"][gr]
                ps = [P2, P3][gr]
                if qt == 0:
                    use_psum("tensor", pn)
                def _wmm(t, idx=idx, gr=gr, ps=ps, hc0=hc0, qt=qt):
                    ins = None
                    nhc = NHCQ[qt]
                    for j in range(nhc):
                        ins = t.matmul(ps[:, 0:512],
                                       tq2[idx % 2][:, 128 * j:128 * (j + 1)],
                                       g_t[hc0 + j][:, 512 * gr:512 * (gr + 1)],
                                       start=(qt == 0 and j == 0),
                                       stop=(qt == 3 and j == nhc - 1))
                    ins.then_inc(SEM["ten"], 1)
                em("tensor", _wmm)
                bump("ten")
            ten_w2[idx] = cnt["ten"]
        for gr in range(2):
            wt("scalar", "ten", ten_w2[4 * ob + 3])
            pn = ["P2", "P3"][gr]
            def _ez(s, ob=ob, gr=gr):
                s.activation(z2[ob][:, 512 * gr:512 * (gr + 1)], [P2, P3][gr][:],
                             AF.Copy, scale=sclc[:, 1:2])
                s.activation(z2[ob][:, 512 * gr:512 * (gr + 1)],
                             z2[ob][:, 512 * gr:512 * (gr + 1)], AF.Copy,
                             scale=sth[:, 4:5]).then_inc(SEM["sca"], 1)
            em("scalar", _ez)
            sca_z2[(ob, gr)] = bump("sca")
            set_pfree([pn], "sca", sca_z2[(ob, gr)])
        wt("scalar", "sca", sca_z2[(ob, 1)])
        if ob >= 2:
            wt("scalar", "ten", ten_z4)
        em("scalar", lambda s, ob=ob: s.activation(
            sq[ob % 2][:, 0:TM], z2[ob][:], AF.Square).then_inc(SEM["sca"], 1))
        schk = bump("sca")
        wt("tensor", "sca", schk)
        if ob == 0:
            use_psum("tensor", "P4", "P5", "P0")
        def _st4(t, ob=ob):
            t.matmul(P4[0:1, 0:512], onesc[:], z2[ob][:, 0:512], start=(ob == 0), stop=(ob == 15))
            t.matmul(P5[0:1, 0:512], onesc[:], z2[ob][:, 512:1024], start=(ob == 0), stop=(ob == 15))
            t.matmul(P0[0:1, 0:512], onesc[:], sq[ob % 2][:, 0:512], start=(ob == 0), stop=(ob == 15))
            t.matmul(P0[0:1, 512:1024], onesc[:], sq[ob % 2][:, 512:1024],
                     start=(ob == 0), stop=(ob == 15)).then_inc(SEM["ten"], 1)
        em("tensor", _st4)
        ten_z4 = bump("ten")

    if debug:
        for ob in range(16):
            wt("sync", "ten", ten_z4)
            em("sync", lambda s, ob=ob: s.dma_start(
                out=dbg["dbg_z2"][128 * ob:128 * (ob + 1), :], in_=z2[ob][:]).then_inc(SEM["out"], 16))
            bump("out", 16)
    rows4, vec_rep4 = ln_rows(TM, [P4[0:1, 0:512], P5[0:1, 0:512]],
                              [P0[0:1, 0:512], P0[0:1, 512:1024]], ten_z4,
                              rowca, rowcb, REPc1, REPc2)
    set_pfree(["P0", "P4", "P5"], "vec", rows4)

    for i in range(16):
        wt("vector", "vec", vec_rep4)
        tf = [tfa, tfb][i % 2]
        sl = "sto%d" % (i % 2)
        if i >= 2:
            wt("vector", sl, 16 * (i // 2))
        def _x3(v, i=i, tf=tf):
            v.tensor_mul(tf[:], z2[i][:], REPc1[:, 0:TM])
            v.tensor_sub(tf[:], tf[:], REPc2[:, 0:TM])
            v.tensor_add(tf[:], x2[i][:], tf[:])
            v.tensor_copy(sacct[:, 0:1], tf[:, 0:1]).then_inc(SEM["vec"], 1)
        em("vector", _x3)
        vchk = bump("vec")
        wt("sync", "vec", vchk)
        em("sync", lambda s, i=i, tf=tf, sl=sl: s.dma_start(
            out=out_d[128 * i:128 * (i + 1), :], in_=tf[:]).then_inc(SEM[sl], 16))
        bump(sl, 16)
    wt("sync", "sto0", cnt["sto0"])
    wt("sync", "sto1", cnt["sto1"])
    if cnt["out"] > 0:
        wt("sync", "out", cnt["out"])

    block = ctx.enter_context(nc.Block())

    @block.tensor
    def _(e):
        for f in streams["tensor"]:
            f(e)

    @block.vector
    def _(e):
        for f in streams["vector"]:
            f(e)

    @block.scalar
    def _(e):
        for f in streams["scalar"]:
            f(e)

    @block.sync
    def _(e):
        for f in streams["sync"]:
            f(e)

    @block.gpsimd
    def _(e):
        for f in streams["gpsimd"]:
            f(e)

    import re as _re
    actual = {}
    sample = {}
    for ins in nc.all_instructions():
        s = str(ins)
        for m in _re.finditer(r"update:S\[(\w+)\]\+[+=]\??(\d*)", s):
            nm, val = m.group(1), m.group(2)
            inc = int(val) if val else 1
            actual[nm] = actual.get(nm, 0) + inc
            sample.setdefault(nm, s[:160])
    for nm, v in cnt.items():
        a = actual.get(nm, 0)
        if a != v:
            print("AUDIT sample for", nm, ":", sample.get(nm, "NONE"))
        assert a == v, f"sem audit mismatch {nm}: emitted {a} vs counted {v}"
    ctx_d.close()
    ctx_c2.close()
    ctx_o.close()
    ctx.close()
    return nc


def _tile_colblocks(WT, ncol, blk):
    Dd, C = WT.shape
    kc = Dd // 128
    nb = C // blk
    r = WT.reshape(kc, 128, nb, blk).transpose(2, 1, 0, 3).reshape(nb, 128, kc * blk)
    return np.ascontiguousarray(r)


def prepare_inputs(L0_attn_x, L0_attn_w_project, L0_attn_w_out, L0_ffn_w1, L0_ffn_w3, L0_ffn_w2):
    x = np.asarray(L0_attn_x, np.float32)
    wp = np.asarray(L0_attn_w_project, np.float32)
    wo = np.asarray(L0_attn_w_out, np.float32)
    w1 = np.asarray(L0_ffn_w1, np.float32)
    w3 = np.asarray(L0_ffn_w3, np.float32)
    w2 = np.asarray(L0_ffn_w2, np.float32)

    WpT = np.ascontiguousarray(wp.T)
    WoT = np.ascontiguousarray(wo.T)
    w1p = np.zeros((HIDP, D), np.float32)
    w1p[:HID] = w1
    w3p = np.zeros((HIDP, D), np.float32)
    w3p[:HID] = w3
    w2p = np.zeros((D, HIDP), np.float32)
    w2p[:, :HID] = w2
    W1T = np.ascontiguousarray(w1p.T)
    W3T = np.ascontiguousarray(w3p.T)
    W2T = np.ascontiguousarray(w2p.T)

    quarters = []
    for Q in range(4):
        for base in (0, 2048, 4096):
            cols = WpT[:, base + 512 * Q: base + 512 * Q + 512]
            unit = _tile_colblocks(cols, 512, 512)[0]
            quarters.extend(np.split(unit, 4, axis=1))
    wp_t = np.ascontiguousarray(np.stack(quarters))
    wo_units = _tile_colblocks(WoT, D, 512)
    wout_t = np.ascontiguousarray(
        np.stack([q for u in range(4) for q in np.split(wo_units[u], 4, axis=1)]))
    w1_t = _tile_colblocks(W1T, HIDP, 128)
    w3_t = _tile_colblocks(W3T, HIDP, 128)
    w2_blocks = _tile_colblocks(W2T, D, 128)
    w2q = np.zeros((16, 4, 128, 1408), np.float32)
    offs = [0, 1408, 2816, 4224]
    for qt in range(4):
        w = [1408, 1408, 1408, 1280][qt]
        w2q[:, qt, :, 0:w] = w2_blocks[:, :, offs[qt]:offs[qt] + w]
    w2q = np.ascontiguousarray(w2q.reshape(64, 128, 1408))

    bf = ml_dtypes.bfloat16
    ident = np.eye(128, dtype=np.float32).astype(bf)

    in_maps = []
    for c in range(8):
        b, g = c // 2, c % 2
        ordr = [2 * j + g for j in range(8)] + [2 * m + 1 - g for m in range(8)]
        xT = np.ascontiguousarray(x[b].T)
        x_loc = np.ascontiguousarray(
            xT.reshape(D, 16, 128)[:, ordr, :].reshape(D, T))
        tri = np.where(np.arange(128)[None, :] >= np.arange(128)[:, None], 0.0, NEG)
        flat = np.zeros((128, 128), np.float32) if g == 1 else np.full((128, 128), NEG, np.float32)
        masks = np.stack([tri, flat]).astype(bf)
        in_maps.append({
            "x_loc": x_loc, "masks": masks, "ident": ident,
            "wp": wp_t, "wout": wout_t, "w1t": w1_t, "w3t": w3_t,
            "w2q": w2q,
        })
    return in_maps


def assemble_output(results):
    out = np.zeros((4, T, D), np.float32)
    for c in range(8):
        b, g = c // 2, c % 2
        o = np.asarray(results[c]["out"], np.float32)
        blocks = o.reshape(D, 8, 128)
        for j in range(8):
            gb = 2 * j + g
            out[b, 128 * gb:128 * (gb + 1), :] = blocks[:, j, :].T
    return out


def kernel(L0_attn_x, L0_attn_w_project, L0_attn_w_out, L0_ffn_w1, L0_ffn_w3, L0_ffn_w2):
    in_maps = prepare_inputs(L0_attn_x, L0_attn_w_project, L0_attn_w_out,
                             L0_ffn_w1, L0_ffn_w3, L0_ffn_w2)
    nc = build_nc(debug=False)
    res = run_bass_kernel_spmd(nc, in_maps, core_ids=list(range(8)))
    return assemble_output(res.results)


# version 11 (final)
# speedup vs baseline: 1.0215x; 1.0215x over previous
import numpy as np
import ml_dtypes
from contextlib import ExitStack
import concourse.bass as bass
import concourse.mybir as mybir
from concourse.bass_utils import run_bass_kernel_spmd

F32 = mybir.dt.float32
BF16 = mybir.dt.bfloat16
AF = mybir.ActivationFunctionType
ALU = mybir.AluOpType
AX = mybir.AxisListType

D = 2048
T = 2048
TM = 1024
HID = 5461
HIDP = 5504
HC = 43
KC = 16
NEG = -1.0e9
EPS = 1e-5

ENGS = ("tensor", "vector", "scalar", "sync", "gpsimd")


def build_nc(debug=False):
    nc = bass.Bass("TRN2", target_bir_lowering=False, detect_race_conditions=False)

    x_loc = nc.dram_tensor("x_loc", [D, T], F32, kind="ExternalInput").ap()
    masks_d = nc.dram_tensor("masks", [2, 128, 128], BF16, kind="ExternalInput").ap()
    ident_d = nc.dram_tensor("ident", [128, 128], BF16, kind="ExternalInput").ap()
    wp_d = nc.dram_tensor("wp", [48, 128, 2048], F32, kind="ExternalInput").ap()
    wout_d = nc.dram_tensor("wout", [16, 128, 2048], F32, kind="ExternalInput").ap()
    w1_d = nc.dram_tensor("w1t", [HC, 128, 2048], F32, kind="ExternalInput").ap()
    w3_d = nc.dram_tensor("w3t", [HC, 128, 2048], F32, kind="ExternalInput").ap()
    w2q_d = nc.dram_tensor("w2q", [64, 128, 1408], F32, kind="ExternalInput").ap()
    out_d = nc.dram_tensor("out", [D, TM], F32, kind="ExternalOutput").ap()
    sbounce = nc.dram_tensor("sbounce", [128, 3], F32).ap()

    dbg = {}
    if debug:
        for nm, shape, dt in [("dbg_y", [D, T], BF16), ("dbg_O", [D, TM], BF16),
                              ("dbg_z", [D, TM], BF16), ("dbg_x2", [D, TM], BF16),
                              ("dbg_s", [128, 6], F32), ("dbg_nb", [128, 16], F32),
                              ("dbg_g", [HIDP, TM], BF16),
                              ("dbg_z2", [D, TM], BF16),
                              ("dbg_sacc", [128, 3], F32), ("dbg_sbf", [128, 16], BF16),
                              ("dbg_q", [512, TM], BF16), ("dbg_k", [512, T], BF16),
                              ("dbg_v", [T, 512], BF16),
                              ("dbg_R1", [128, TM], BF16), ("dbg_R2", [128, TM], BF16),
                              ("dbg_y2", [D, TM], BF16),
                              ("dbg_ra", [1, TM], F32), ("dbg_rb", [1, TM], F32)]:
            dbg[nm] = nc.dram_tensor(nm, shape, dt, kind="ExternalOutput").ap()

    ctx = ExitStack()
    ctx_ab = ExitStack()
    ctx_c1 = ExitStack()
    ctx_c2 = ExitStack()
    ctx_d = ExitStack()

    def sb(st, n, s, dt):
        return st.enter_context(nc.sbuf_tensor(n, s, dt))

    onesr = sb(ctx, "onesr", [1, 128], BF16)
    onesc = sb(ctx, "onesc", [128, 1], BF16)
    consts = sb(ctx, "consts", [1, 8], F32)
    sth = sb(ctx, "sth", [128, 6], F32)
    sclc = sb(ctx, "sclc", [128, 2], F32)
    cinv = sb(ctx, "cinv", [128, 1], F32)
    negone = sb(ctx, "negone", [128, 1], F32)
    c128i = sb(ctx, "c128i", [128, 1], F32)
    chalf = sb(ctx, "chalf", [128, 1], F32)
    btmp = sb(ctx, "btmp", [128, 2], F32)
    drn = sb(ctx, "drn", [1, 4], F32)
    sacc = sb(ctx, "sacc", [128, 3], F32)
    sacct = sb(ctx, "sacct", [128, 2], F32)
    sbf = sb(ctx, "sbf", [128, 8], BF16)
    sbflo = sb(ctx, "sbflo", [128, 8], BF16)
    sfull = sb(ctx, "sfull", [128, 3], F32)
    ones128 = sb(ctx, "ones128", [128, 128], BF16)
    bigrow = sb(ctx, "bigrow", [1, 512], BF16)
    cs2 = sb(ctx, "cs2", [128, 1], F32)
    csn = sb(ctx, "csn", [128, 1], F32)
    rowbf = sb(ctx, "rowbf", [1, 1024], BF16)
    vbf = None
    rowa = sb(ctx, "rowa", [1, 1024], F32)
    rowb = sb(ctx, "rowb", [1, 1024], F32)
    REP1 = sb(ctx, "REP1", [128, 1024], BF16)
    REP2 = sb(ctx, "REP2", [128, 1024], BF16)
    sq = [sb(ctx, f"sq{i}", [128, 2048], BF16) for i in range(2)]
    strm = [sb(ctx, f"strm{i}", [128, 2048], F32) for i in range(2)]
    ctx_o = ExitStack()
    O = [sb(ctx_o, f"Ot{i}", [128, TM], BF16) for i in range(16)]
    ident = sb(ctx_ab, "identt", [128, 128], BF16)
    negB = sb(ctx_ab, "negB", [128, 16], F32)
    nrm = sb(ctx_ab, "nrm", [1, 8], F32)

    ptil = [sb(ctx_ab, f"pt{i}", [128, 1024], BF16) for i in range(2)]
    maskt = sb(ctx_ab, "maskt", [128, 256], BF16)
    y = [sb(ctx_ab, f"y{k}", [128, T], BF16) for k in range(KC)]
    qq = [sb(ctx_ab, f"q{i}", [128, TM], BF16) for i in range(4)]
    kq = [sb(ctx_ab, f"k{i}", [128, T], BF16) for i in range(4)]
    vq = [sb(ctx_ab, f"v{i}", [128, 512], BF16) for i in range(16)]
    wbuf = [sb(ctx_ab, f"wb{i}", [128, 2048], BF16) for i in range(5)]

    P0 = ctx.enter_context(nc.psum_tensor("P0", [128, 1024], F32))
    P1 = ctx.enter_context(nc.psum_tensor("P1", [128, 1024], F32))
    P2 = ctx.enter_context(nc.psum_tensor("P2", [128, 512], F32))
    P3 = ctx.enter_context(nc.psum_tensor("P3", [128, 512], F32))
    P4 = ctx.enter_context(nc.psum_tensor("P4", [128, 512], F32))
    P5 = ctx.enter_context(nc.psum_tensor("P5", [128, 512], F32))

    sems = {}
    for s in ("ten", "vec", "sca", "gpm", "dgc", "dgx", "dgb", "dxm", "out", "stm",
              "wps0", "wps1", "wps2", "wps3", "wps4", "sts0", "sts1", "stg1", "sto0", "sto1"):
        sems[s] = ctx.enter_context(nc.semaphore(s))
    cnt = {s: 0 for s in sems}
    streams = {e: [] for e in ENGS}
    SEM = sems

    def em(e, f):
        streams[e].append(f)

    def wt(e, s, v):
        if v and v > 0:
            h = sems[s]
            streams[e].append(lambda eng, h=h, v=v: eng.wait_ge(h, v))

    def bump(s, n=1):
        cnt[s] += n
        return cnt[s]

    pfree = {}

    def use_psum(e, *names):
        for n in names:
            d = pfree.get(n)
            if d:
                wt(e, d[0], d[1])

    def set_pfree(names, s, v):
        for n in names:
            pfree[n] = (s, v)

    def _init(g):
        g.memset(onesr[:], 1.0)
        g.memset(onesc[:], 1.0)
        g.memset(consts[0:1, 0:1], 1.0 / D)
        g.memset(consts[0:1, 1:2], EPS)
        g.memset(consts[0:1, 2:3], -0.5)
        g.memset(consts[0:1, 3:4], 1.0 / 128.0)
        g.memset(consts[0:1, 4:5], 0.5 / (HID * D))
        g.memset(consts[0:1, 5:6], 0.5 / (HID * D))
        g.memset(consts[0:1, 6:7], 0.5)
        g.memset(sacc[:], 0.0)
        g.memset(sbf[:], 0.0)
        g.memset(sbflo[:], 0.0)
        g.memset(ones128[:], 1.0)
        g.memset(cs2[:], 0.5 / (HID * D))
        g.memset(csn[:], -0.5 / (HID * D))
        g.memset(negone[:], -1.0)
        g.memset(c128i[:], 1.0 / 128.0)
        g.memset(chalf[:], 0.5)
        g.memset(cinv[:], 1.0 / (128.0 ** 0.5)).then_inc(SEM["gpm"], 1)
    em("gpsimd", _init)
    gpm_init = bump("gpm")
    em("gpsimd", lambda g: g.dma_start(out=ident[:], in_=ident_d).then_inc(SEM["dgc"], 16))
    em("gpsimd", lambda g: g.dma_start(out=maskt[:, 0:128], in_=masks_d[0]).then_inc(SEM["dgc"], 16))
    em("gpsimd", lambda g: g.dma_start(out=maskt[:, 128:256], in_=masks_d[1]).then_inc(SEM["dgc"], 16))
    bump("dgc", 48)
    for k in range(KC):
        em("gpsimd", lambda g, k=k: g.dma_start(
            out=y[k][:], in_=x_loc[128 * k:128 * (k + 1), :]).then_inc(SEM["dgx"], 16))
        bump("dgx", 16)

    ten_st = 0
    for k in range(KC):
        wt("scalar", "dgx", 256)
        wt("scalar", "gpm", gpm_init)
        if k >= 2:
            wt("scalar", "ten", ten_st - 1)
        em("scalar", lambda s, k=k: s.activation(
            sq[k % 2][:], y[k][:], AF.Square).then_inc(SEM["sca"], 1))
        schk = bump("sca")
        wt("tensor", "sca", schk)
        if k == 0:
            use_psum("tensor", "P0", "P1", "P2", "P3", "P4", "P5")
        def _stat(t, k=k):
            for gr in range(4):
                pr = (P0 if gr < 2 else P1)[0:1, 512 * (gr % 2):512 * (gr % 2) + 512]
                t.matmul(pr, onesc[:], y[k][:, 512 * gr:512 * (gr + 1)],
                         start=(k == 0), stop=(k == KC - 1))
            ins = None
            for gr in range(4):
                pr = [P2, P3, P4, P5][gr][0:1, 0:512]
                ins = t.matmul(pr, onesc[:], sq[k % 2][:, 512 * gr:512 * (gr + 1)],
                               start=(k == 0), stop=(k == KC - 1))
            ins.then_inc(SEM["ten"], 1)
        em("tensor", _stat)
        ten_st = bump("ten")

    def ln_rows(Tn, psx_rows, psq_rows, ten_dep, rowa, rowb, REP1, REP2):
        ng = Tn // 512
        wt("vector", "ten", ten_dep)
        def _rows(v):
            for gr in range(ng):
                v.tensor_single_scalar(rowa[0:1, 512 * gr:512 * (gr + 1)], psx_rows[gr],
                                       consts[0:1, 0:1], ALU.mult)
            v.tensor_mul(rowb[0:1, 0:Tn], rowa[0:1, 0:Tn], rowa[0:1, 0:Tn])
            for gr in range(ng):
                v.scalar_tensor_tensor(rowb[0:1, 512 * gr:512 * (gr + 1)], psq_rows[gr],
                                       consts[0:1, 0:1], rowb[0:1, 512 * gr:512 * (gr + 1)],
                                       ALU.mult, ALU.subtract)
            v.tensor_copy(rowbf[0:1, 0:Tn], rowb[0:1, 0:Tn])
            v.tensor_copy(drn[0:1, 0:2], rowbf[0:1, Tn - 2:Tn]).then_inc(SEM["vec"], 1)
        em("vector", _rows)
        vchk = bump("vec")
        rows_chk = vchk
        wt("tensor", "vec", vchk)
        use_psum("tensor", "P2", "P3")
        def _vmm(t):
            t.matmul(P2[0:1, 0:512], onesc[0:1, 0:1], rowbf[0:1, 0:512], start=True, stop=True)
            t.matmul(P3[0:1, 0:512], onesc[0:1, 0:1], rowbf[0:1, 512:1024],
                     start=True, stop=True).then_inc(SEM["ten"], 1)
        em("tensor", _vmm)
        tvchk = bump("ten")
        wt("scalar", "ten", tvchk)
        def _rstd(s):
            s.activation(rowb[0:1, 0:512], P2[0:1, 0:512], AF.Ln, bias=consts[0:1, 1:2])
            s.activation(rowb[0:1, 512:1024], P3[0:1, 0:512], AF.Ln, bias=consts[0:1, 1:2])
            s.activation(rowb[0:1, 0:Tn], rowb[0:1, 0:Tn], AF.Exp, scale=consts[0:1, 2:3])
            s.activation(drn[0:1, 2:3], rowb[0:1, 0:1], AF.Copy).then_inc(SEM["sca"], 1)
        em("scalar", _rstd)
        schk = bump("sca")
        set_pfree(["P2", "P3"], "sca", schk)
        wt("vector", "sca", schk)
        def _casts(v):
            v.tensor_copy(rowbf[0:1, 0:Tn], rowb[0:1, 0:Tn])
            v.tensor_mul(sq[1][0:1, 0:Tn], rowa[0:1, 0:Tn], rowb[0:1, 0:Tn])
            v.tensor_copy(drn[0:1, 0:2], sq[1][0:1, Tn - 2:Tn]).then_inc(SEM["vec"], 1)
        em("vector", _casts)
        vchk2 = bump("vec")
        last_v = 0
        for gr in range(ng):
            wt("tensor", "vec", vchk2)
            use_psum("tensor", "P2", "P3")
            def _bc(t, gr=gr):
                t.matmul(P2[:, 0:512], onesr[:], rowbf[0:1, 512 * gr:512 * (gr + 1)],
                         start=True, stop=True)
                t.matmul(P3[:, 0:512], onesr[:], sq[1][0:1, 512 * gr:512 * (gr + 1)],
                         start=True, stop=True).then_inc(SEM["ten"], 1)
            em("tensor", _bc)
            tchk = bump("ten")
            wt("vector", "ten", tchk)
            def _ev(v, gr=gr):
                v.tensor_copy(REP1[:, 512 * gr:512 * (gr + 1)], P2[:])
                v.tensor_copy(REP2[:, 512 * gr:512 * (gr + 1)], P3[:]).then_inc(SEM["vec"], 1)
            em("vector", _ev)
            last_v = bump("vec")
            set_pfree(["P2", "P3"], "vec", last_v)
        return rows_chk, last_v

    set_pfree(["P0", "P1", "P2", "P3", "P4", "P5"], None, 0)
    for half in range(2):
        psx = [P0[0:1, 0:512], P0[0:1, 512:1024]] if half == 0 else \
              [P1[0:1, 0:512], P1[0:1, 512:1024]]
        psq = [P2[0:1, 0:512], P3[0:1, 0:512]] if half == 0 else \
              [P4[0:1, 0:512], P5[0:1, 0:512]]
        rows1, vec_rep1 = ln_rows(TM, psx, psq, ten_st, rowa, rowb, REP1, REP2)
        if half == 1:
            set_pfree(["P0", "P1", "P4", "P5"], "vec", rows1)
        for k in range(KC):
            wt("vector", "vec", vec_rep1)
            def _ap(v, k=k, half=half):
                c0, c1 = 1024 * half, 1024 * (half + 1)
                v.tensor_mul(y[k][:, c0:c1], y[k][:, c0:c1], REP1[:, 0:TM])
                v.tensor_sub(y[k][:, c0:c1], y[k][:, c0:c1],
                             REP2[:, 0:TM]).then_inc(SEM["vec"], 1)
            em("vector", _ap)
            bump("vec")
    vec_ydone = cnt["vec"]

    if debug:
        for k in range(KC):
            wt("sync", "vec", vec_ydone)
            em("sync", lambda s, k=k: s.dma_start(
                out=dbg["dbg_y"][128 * k:128 * (k + 1), :], in_=y[k][:]).then_inc(SEM["out"], 16))
            bump("out", 16)
        em("sync", lambda s: s.dma_start(out=dbg["dbg_R1"], in_=REP1[:, 0:TM]).then_inc(SEM["out"], 16))
        bump("out", 16)
        em("sync", lambda s: s.dma_start(out=dbg["dbg_R2"], in_=REP2[:, 0:TM]).then_inc(SEM["out"], 16))
        bump("out", 16)
        em("sync", lambda s: s.dma_start(out=dbg["dbg_ra"], in_=rowa[0:1, 0:TM]).then_inc(SEM["out"], 16))
        bump("out", 16)
        em("sync", lambda s: s.dma_start(out=dbg["dbg_rb"], in_=rowb[0:1, 0:TM]).then_inc(SEM["out"], 16))
        bump("out", 16)
        wt("gpsimd", "out", cnt["out"])
        em("gpsimd", lambda g: g.memset(consts[0:1, 7:8], 0.0).then_inc(SEM["stm"], 1))
        bump("stm")

    scan_list = []
    for i in range(HC):
        scan_list.append((w1_d, i, 2048, 0))
    for i in range(HC):
        scan_list.append((w3_d, i, 2048, 1))
    for i in range(64):
        scan_list.append((w2q_d, i, 1408, 2))
    n_scan = len(scan_list)
    sstate = {"ld": 0, "rd": 0, "dsy": [], "vchk": []}

    def scan_load(n):
        for _ in range(n):
            i = sstate["ld"]
            if i >= n_scan:
                return
            dram, idx, w, col = scan_list[i]
            if i >= 2:
                wt("sync", "vec", sstate["vchk"][i - 2])
            sl = "sts%d" % (i % 2)
            em("sync", lambda s, dram=dram, idx=idx, w=w, i=i, sl=sl: s.dma_start(
                out=strm[i % 2][:, 0:w], in_=dram[idx]).then_inc(SEM[sl], 16))
            sstate["dsy"].append(bump(sl, 16))
            sstate["ld"] += 1

    def scan_red(n):
        for _ in range(n):
            i = sstate["rd"]
            if i >= n_scan or i >= sstate["ld"]:
                return
            dram, idx, w, col = scan_list[i]
            wt("vector", "sts%d" % (i % 2), sstate["dsy"][i])
            pcol = scan_list[i - 1][3] if i >= 1 else None
            def _r(v, w=w, i=i, pcol=pcol):
                v.reduce_sum(sacct[:, i % 2:i % 2 + 1], strm[i % 2][:, 0:w], axis=AX.X,
                             apply_absolute_value=True)
                ins = v.tensor_add(sacc[:, pcol:pcol + 1], sacc[:, pcol:pcol + 1],
                                   sacct[:, (i - 1) % 2:(i - 1) % 2 + 1]) if pcol is not None else \
                    v.tensor_copy(drn[0:1, 0:2], consts[0:1, 0:2])
                ins.then_inc(SEM["vec"], 1)
            em("vector", _r)
            sstate["vchk"].append(bump("vec"))
            sstate["rd"] += 1

    def scan_flush():
        i = n_scan - 1
        col = scan_list[i][3]
        def _rf(v, i=i, col=col):
            v.tensor_copy(ptil[0][:, 0:512], ptil[0][:, 0:512])
            v.tensor_copy(ptil[0][:, 512:1024], ptil[0][:, 512:1024])
            v.tensor_add(sacc[:, col:col + 1], sacc[:, col:col + 1],
                         sacct[:, i % 2:i % 2 + 1]).then_inc(SEM["vec"], 1)
        em("vector", _rf)
        return bump("vec")

    wp_dgp = {}
    q_done = {}
    wp_state = {"next": 0}

    def wp_fill(limit):
        while wp_state["next"] < min(48, limit):
            i = wp_state["next"]
            if i >= 5:
                wt("gpsimd", "ten", q_done[i - 5])
            sl = "wps%d" % (i % 5)
            em("gpsimd", lambda g, i=i, sl=sl: g.dma_start(
                out=wbuf[i % 5][:], in_=wp_d[i]).then_inc(SEM[sl], 16))
            wp_dgp[i] = (sl, bump(sl, 16))
            wp_state["next"] += 1

    wp_fill(5)

    sca_exp = {}
    ten_pv = {}
    vec_fin = {}

    if debug:
        wt("vector", "stm", cnt["stm"])
    for Q in range(4):
        bq = 12 * Q
        wt("tensor", wp_dgp[bq + 3][0], wp_dgp[bq + 3][1])
        wt("tensor", wp_dgp[bq + 0][0], wp_dgp[bq + 0][1])
        wt("tensor", wp_dgp[bq + 1][0], wp_dgp[bq + 1][1])
        wt("tensor", wp_dgp[bq + 2][0], wp_dgp[bq + 2][1])
        wt("tensor", "dgc", 48)
        wt("tensor", "vec", vec_ydone)
        for hh in range(4):
            for gr in range(2):
                pn = ["P2", "P3"][gr]
                ps = [P2, P3][gr]
                use_psum("tensor", pn)
                def _qmm(t, hh=hh, gr=gr, bq=bq, ps=ps):
                    ins = None
                    for kd in range(KC):
                        ins = t.matmul(ps[:, 0:512],
                                       wbuf[(bq + kd // 4) % 5][:, 512 * (kd % 4) + 128 * hh:512 * (kd % 4) + 128 * hh + 128],
                                       y[kd][:, 512 * gr:512 * (gr + 1)],
                                       start=(kd == 0), stop=(kd == KC - 1))
                    ins.then_inc(SEM["ten"], 1)
                em("tensor", _qmm)
                tchk = bump("ten")
                wt("vector", "ten", tchk)
                em("vector", lambda v, hh=hh, gr=gr, ps=ps: v.tensor_copy(
                    qq[hh][:, 512 * gr:512 * (gr + 1)], ps[:]).then_inc(SEM["vec"], 1))
                set_pfree([pn], "vec", bump("vec"))
        for j in range(4):
            q_done[bq + j] = cnt["ten"]
        wp_fill(bq + 9)
        for j in range(4, 8):
            wt("tensor", wp_dgp[bq + j][0], wp_dgp[bq + j][1])
        for hh in range(4):
            for gr in range(4):
                pn = ["P2", "P3", "P4", "P5"][gr]
                ps = [P2, P3, P4, P5][gr]
                use_psum("tensor", pn)
                def _kmm(t, hh=hh, gr=gr, bq=bq, ps=ps):
                    ins = None
                    for kd in range(KC):
                        ins = t.matmul(ps[:, 0:512],
                                       wbuf[(bq + 4 + kd // 4) % 5][:, 512 * (kd % 4) + 128 * hh:512 * (kd % 4) + 128 * hh + 128],
                                       y[kd][:, 512 * gr:512 * (gr + 1)],
                                       start=(kd == 0), stop=(kd == KC - 1))
                    ins.then_inc(SEM["ten"], 1)
                em("tensor", _kmm)
                tchk = bump("ten")
                wt("vector", "ten", tchk)
                em("vector", lambda v, hh=hh, gr=gr, ps=ps: v.tensor_copy(
                    kq[hh][:, 512 * gr:512 * (gr + 1)], ps[:]).then_inc(SEM["vec"], 1))
                set_pfree([pn], "vec", bump("vec"))
        for j in range(4, 8):
            q_done[bq + j] = cnt["ten"]
        wp_fill(bq + 13)
        for j in range(8, 12):
            wt("tensor", wp_dgp[bq + j][0], wp_dgp[bq + j][1])
        for tc in range(16):
            pn = ["P2", "P3"][tc % 2]
            ps = [P2, P3][tc % 2]
            use_psum("tensor", pn)
            def _vmm(t, tc=tc, bq=bq, ps=ps):
                ins = None
                for kd in range(KC):
                    ins = t.matmul(ps[:, 0:512],
                                   y[kd][:, 128 * tc:128 * (tc + 1)],
                                   wbuf[(bq + 8 + kd // 4) % 5][:, 512 * (kd % 4):512 * (kd % 4) + 512],
                                   start=(kd == 0), stop=(kd == KC - 1))
                ins.then_inc(SEM["ten"], 1)
            em("tensor", _vmm)
            tchk = bump("ten")
            wt("vector", "ten", tchk)
            em("vector", lambda v, tc=tc, ps=ps: v.tensor_copy(
                vq[tc][:], ps[:]).then_inc(SEM["vec"], 1))
            set_pfree([pn], "vec", bump("vec"))
        for j in range(8, 12):
            q_done[bq + j] = cnt["ten"]
        wp_fill(bq + 17)
        for _ in range(19):
            scan_load(1)
            scan_red(1)

        for hh in range(4):
            h = 4 * Q + hh
            wt("scalar", "vec", cnt["vec"])
            def _sqr(s, hh=hh):
                s.activation(sq[0][:, 0:1024], qq[hh][:], AF.Square)
                s.activation(sq[1][:], kq[hh][:], AF.Square).then_inc(SEM["sca"], 1)
            em("scalar", _sqr)
            schk = bump("sca")
            wt("tensor", "sca", schk)
            use_psum("tensor", "P2", "P3", "P4", "P5", "P0")
            def _nmm(t):
                t.matmul(P4[0:1, 0:512], onesc[:], sq[0][:, 0:512], start=True, stop=True)
                t.matmul(P5[0:1, 0:512], onesc[:], sq[0][:, 512:1024], start=True, stop=True)
                t.matmul(P2[0:1, 0:512], onesc[:], sq[1][:, 0:512], start=True, stop=True)
                t.matmul(P3[0:1, 0:512], onesc[:], sq[1][:, 512:1024], start=True, stop=True)
                t.matmul(P0[0:1, 0:512], onesc[:], sq[1][:, 1024:1536], start=True, stop=True)
                t.matmul(P0[0:1, 512:1024], onesc[:], sq[1][:, 1536:2048],
                         start=True, stop=True).then_inc(SEM["ten"], 1)
            em("tensor", _nmm)
            tchk = bump("ten")
            wt("vector", "ten", tchk)
            def _bv(v, h=h, hh=hh):
                v.tensor_copy(rowa[0:1, 0:512], P4[0:1, 0:512])
                v.tensor_copy(rowa[0:1, 512:1024], P5[0:1, 0:512])
                v.reduce_max(nrm[0:1, 0:1], rowa[0:1, 0:1024], axis=AX.X)
                v.tensor_copy(rowa[0:1, 0:512], P2[0:1, 0:512])
                v.tensor_copy(rowa[0:1, 512:1024], P3[0:1, 0:512])
                v.reduce_max(nrm[0:1, 4:5], rowa[0:1, 0:1024], axis=AX.X)
                v.tensor_copy(rowa[0:1, 0:512], P0[0:1, 0:512])
                v.tensor_copy(rowa[0:1, 512:1024], P0[0:1, 512:1024])
                v.reduce_max(nrm[0:1, 5:6], rowa[0:1, 0:1024], axis=AX.X)
                v.tensor_copy(drn[0:1, 0:2], rowa[0:1, 0:2])
                v.tensor_max(nrm[0:1, 1:2], nrm[0:1, 4:5], nrm[0:1, 5:6])
                v.tensor_copy(drn[0:1, 2:4], rowa[0:1, 2:4])
                v.tensor_mul(nrm[0:1, 2:3], nrm[0:1, 0:1], nrm[0:1, 1:2])
                v.tensor_copy(drn[0:1, 0:2], rowa[0:1, 4:6])
                for j in range(4):
                    v.tensor_single_scalar(bigrow[0:1, 128 * j:128 * (j + 1)],
                                           onesr[0:1, 0:128], nrm[0:1, 2:3], ALU.mult)
                v.tensor_copy(drn[0:1, 2:4], bigrow[0:1, 508:510]).then_inc(SEM["vec"], 1)
            em("vector", _bv)
            vchk = bump("vec")
            set_pfree(["P2", "P3", "P4", "P5", "P0"], "vec", vchk)
            wt("tensor", "vec", vchk)
            use_psum("tensor", "P4")
            em("tensor", lambda t: t.matmul(P4[:, 0:512], onesr[:], bigrow[0:1, 0:512],
                                            start=True, stop=True).then_inc(SEM["ten"], 1))
            tchk2 = bump("ten")
            wt("scalar", "ten", tchk2)
            def _bs(s, h=h):
                s.activation(btmp[:, 0:1], P4[:, 0:1], AF.Ln, scale=c128i[:, 0:1])
                s.activation(btmp[:, 1:2], btmp[:, 0:1], AF.Exp, scale=chalf[:, 0:1])
                s.activation(negB[:, h:h + 1], btmp[:, 1:2], AF.Copy,
                             scale=negone[:, 0:1]).then_inc(SEM["sca"], 1)
            em("scalar", _bs)
            schk2 = bump("sca")
            set_pfree(["P4"], "sca", schk2)
        vec_negb = cnt["vec"]

        for hh in range(4):
            h = 4 * Q + hh

            def emit_pv(p, hh=hh, h=h):
                wt("tensor", "sca", sca_exp[(h, p)])
                if p == 0:
                    use_psum("tensor", "P2", "P3", "P4", "P5")
                qs = 128 * (p % 8)
                def _pv(t, hh=hh, p=p, qs=qs):
                    if qs < 512:
                        t.matmul(P2[:, qs:512], vq[p][:, 128 * hh:128 * (hh + 1)],
                                 ptil[p % 2][:, qs:512], start=(p == 0), stop=(p == 11))
                        t.matmul(P4[0:1, qs:512], onesc[:], ptil[p % 2][:, qs:512],
                                 start=(p == 0), stop=(p == 11))
                    gs = max(qs, 512)
                    t.matmul(P3[:, gs - 512:512], vq[p][:, 128 * hh:128 * (hh + 1)],
                             ptil[p % 2][:, gs:1024], start=(p == 0), stop=(p == 15))
                    t.matmul(P5[0:1, gs - 512:512], onesc[:], ptil[p % 2][:, gs:1024],
                             start=(p == 0), stop=(p == 15)).then_inc(SEM["ten"], 1)
                em("tensor", _pv)
                ten_pv[(h, p)] = bump("ten")

            for p in range(16):
                qs = 128 * (p % 8)
                sn = ["P0", "P1"][p % 2]
                sT = [P0, P1][p % 2]
                wt("tensor", "vec", vec_negb)
                use_psum("tensor", sn)
                def _qk(t, hh=hh, p=p, qs=qs, sT=sT):
                    if qs < 512:
                        t.matmul(sT[:, qs:512], kq[hh][:, 128 * p:128 * (p + 1)],
                                 qq[hh][:, qs:512], start=True, stop=False)
                        t.matmul(sT[:, 512:1024], kq[hh][:, 128 * p:128 * (p + 1)],
                                 qq[hh][:, 512:1024], start=True, stop=True)
                    else:
                        t.matmul(sT[:, qs:1024], kq[hh][:, 128 * p:128 * (p + 1)],
                                 qq[hh][:, qs:1024], start=True, stop=False)
                    mk = maskt[:, 0:128] if p < 8 else maskt[:, 128:256]
                    t.matmul(sT[:, qs:qs + 128], ident[:], mk,
                             start=False, stop=True).then_inc(SEM["ten"], 1)
                em("tensor", _qk)
                tchk = bump("ten")
                wt("scalar", "ten", tchk)
                if p >= 2:
                    wt("scalar", "ten", ten_pv.get((h, p - 2), 0))
                elif h > 0:
                    wt("scalar", "vec", vec_fin.get(h - 1, 0))
                def _exp(s, h=h, p=p, qs=qs, sT=sT):
                    s.activation(ptil[p % 2][:, qs:1024], sT[:, qs:1024], AF.Exp,
                                 bias=negB[:, h:h + 1],
                                 scale=cinv[:, 0:1]).then_inc(SEM["sca"], 1)
                em("scalar", _exp)
                sca_exp[(h, p)] = bump("sca")
                set_pfree([sn], "sca", sca_exp[(h, p)])
                if p >= 1:
                    emit_pv(p - 1)
            emit_pv(15)
            wt("vector", "ten", ten_pv[(h, 15)])
            def _fv(v):
                with nc.allow_low_precision(reason="softmax denom recip bf16; LN-invariant"):
                    v.reciprocal(rowbf[0:1, 0:512], P4[0:1, 0:512])
                    v.reciprocal(rowbf[0:1, 512:1024], P5[0:1, 0:512])
                v.tensor_copy(drn[0:1, 0:2], rowbf[0:1, 1022:1024]).then_inc(SEM["vec"], 1)
            em("vector", _fv)
            vchk = bump("vec")
            wt("tensor", "vec", vchk)
            use_psum("tensor", "P0")
            def _ft(t):
                t.matmul(P0[:, 0:512], onesr[:], rowbf[0:1, 0:512], start=True, stop=True)
                t.matmul(P0[:, 512:1024], onesr[:], rowbf[0:1, 512:1024],
                         start=True, stop=True).then_inc(SEM["ten"], 1)
            em("tensor", _ft)
            tchk = bump("ten")
            wt("vector", "ten", tchk)
            def _fv2(v, h=h):
                v.tensor_copy(ptil[0][:, 0:512], P0[:, 0:512])
                v.tensor_copy(ptil[0][:, 512:1024], P0[:, 512:1024])
                v.tensor_mul(O[h][:, 0:512], P2[:], ptil[0][:, 0:512])
                v.tensor_mul(O[h][:, 512:1024], P3[:], ptil[0][:, 512:1024]).then_inc(SEM["vec"], 1)
            em("vector", _fv2)
            vec_fin[h] = bump("vec")
            set_pfree(["P0", "P2", "P3", "P4", "P5"], "vec", vec_fin[h])
        for _ in range(19):
            scan_load(1)
            scan_red(1)

    ten_attn_done = cnt["ten"]
    vec_attn_done = cnt["vec"]
    while sstate["rd"] < n_scan:
        scan_load(1)
        scan_red(1)
    scan_flush()
    vec_scan_done = cnt["vec"]

    if debug:
        for h in range(16):
            wt("sync", "vec", vec_attn_done)
            em("sync", lambda s, h=h: s.dma_start(
                out=dbg["dbg_O"][128 * h:128 * (h + 1), :], in_=O[h][:]).then_inc(SEM["out"], 16))
            bump("out", 16)
        wt("sync", "vec", vec_attn_done)
        em("sync", lambda s: s.dma_start(out=dbg["dbg_nb"], in_=negB[:]).then_inc(SEM["out"], 16))
        bump("out", 16)
        for hh in range(4):
            em("sync", lambda s, hh=hh: s.dma_start(
                out=dbg["dbg_q"][128 * hh:128 * (hh + 1), :], in_=qq[hh][:]).then_inc(SEM["out"], 16))
            bump("out", 16)
            em("sync", lambda s, hh=hh: s.dma_start(
                out=dbg["dbg_k"][128 * hh:128 * (hh + 1), :], in_=kq[hh][:]).then_inc(SEM["out"], 16))
            bump("out", 16)
        for tc in range(16):
            em("sync", lambda s, tc=tc: s.dma_start(
                out=dbg["dbg_v"][128 * tc:128 * (tc + 1), :], in_=vq[tc][:]).then_inc(SEM["out"], 16))
            bump("out", 16)
        wt("gpsimd", "out", cnt["out"])
        em("gpsimd", lambda g: g.memset(consts[0:1, 7:8], 0.0).then_inc(SEM["stm"], 1))
        bump("stm")

    def _sdr(v):
        v.tensor_copy(ptil[0][:, 0:512], ptil[0][:, 0:512])
        v.tensor_copy(ptil[0][:, 512:1024], ptil[0][:, 512:1024])
        v.tensor_copy(sacct[:, 0:1], sacc[:, 0:1]).then_inc(SEM["vec"], 1)
    em("vector", _sdr)
    vec_scan_drain = bump("vec")
    wt("gpsimd", "vec", vec_scan_drain)
    em("gpsimd", lambda g: g.dma_start(out=sbounce, in_=sacc[:]).then_inc(SEM["dgb"], 16))
    d1 = bump("dgb", 16)
    wt("gpsimd", "dgb", d1)
    em("gpsimd", lambda g: g.dma_start(out=sbf[:, 0:3], in_=sbounce).then_inc(SEM["dgb"], 16))
    bump("dgb", 16)
    em("gpsimd", lambda g: g.dma_start(out=sfull[:], in_=sbounce).then_inc(SEM["dgb"], 16))
    d2 = bump("dgb", 16)
    wt("gpsimd", "dgb", d2)
    em("gpsimd", lambda g: g.tensor_sub(sbflo[:, 0:3], sfull[:], sbf[:, 0:3]).then_inc(SEM["gpm"], 1))
    gpm_lo = bump("gpm")
    wt("tensor", "gpm", gpm_lo)
    use_psum("tensor", "P4")
    def _smm(t):
        t.matmul(P4[:, 0:8], ones128[:, 0:128], sbf[:, 0:8], start=True, stop=False)
        t.matmul(P4[:, 0:8], ones128[:, 0:128], sbflo[:, 0:8],
                 start=False, stop=True).then_inc(SEM["ten"], 1)
    em("tensor", _smm)
    tchk = bump("ten")
    wt("scalar", "ten", tchk)
    def _ses(s):
        s.activation(sth[:, 0:1], P4[:, 0:1], AF.Copy, scale=cs2[:, 0:1])
        s.activation(sth[:, 1:2], P4[:, 0:1], AF.Copy, scale=csn[:, 0:1])
        s.activation(sth[:, 2:3], P4[:, 1:2], AF.Copy, scale=cs2[:, 0:1])
        s.activation(sth[:, 3:4], P4[:, 1:2], AF.Copy, scale=csn[:, 0:1])
        s.activation(sth[:, 4:5], P4[:, 2:3], AF.Copy, scale=cs2[:, 0:1])
        s.activation(sth[:, 5:6], P4[:, 2:3], AF.Copy, scale=csn[:, 0:1])
        s.activation(sclc[:, 0:1], P4[:, 0:1], AF.Copy, scale=cs2[:, 0:1])
        s.activation(sclc[:, 1:2], P4[:, 1:2], AF.Copy, scale=cs2[:, 0:1]).then_inc(SEM["sca"], 1)
    em("scalar", _ses)
    sca_scales = bump("sca")
    set_pfree(["P4"], "sca", sca_scales)

    if debug:
        wt("sync", "gpm", gpm_lo)
        em("sync", lambda s: s.dma_start(out=dbg["dbg_sacc"], in_=sacc[:]).then_inc(SEM["out"], 16))
        bump("out", 16)
        em("sync", lambda s: s.dma_start(out=dbg["dbg_sbf"][:, 0:8], in_=sbf[:]).then_inc(SEM["out"], 16))
        bump("out", 16)
        em("sync", lambda s: s.dma_start(out=dbg["dbg_sbf"][:, 8:16], in_=sbflo[:]).then_inc(SEM["out"], 16))
        bump("out", 16)

    if debug:
        wt("sync", "sca", sca_scales)
        em("sync", lambda s: s.dma_start(out=dbg["dbg_s"], in_=sth[:, 0:6]).then_inc(SEM["out"], 16))
        bump("out", 16)

    ctx_ab.close()
    zsb = [sb(ctx_c2, f"z{i}", [128, TM], BF16) for i in range(16)]
    tfa = sb(ctx_c2, "tfa", [128, TM], F32)
    tfb = sb(ctx_c2, "tfb", [128, TM], F32)
    rowca, rowcb, REPc1, REPc2 = rowa, rowb, REP1, REP2
    xm = [sb(ctx_c1, f"xm{i}", [128, TM], F32) for i in range(16)]
    wob = [sb(ctx_c1, f"wo{i}", [128, 2048], BF16) for i in range(5)]

    for i in range(16):
        wt("sync", "ten", ten_attn_done if i == 0 else 0)
        em("sync", lambda s, i=i: s.dma_start(
            out=xm[i][:], in_=x_loc[128 * i:128 * (i + 1), 0:TM]).then_inc(SEM["dxm"], 16))
        bump("dxm", 16)
    wo_dgp = {}
    wo_done = {}
    wo_state = {"next": 0}

    def wo_fill(limit):
        while wo_state["next"] < min(16, limit):
            i = wo_state["next"]
            if i == 0:
                wt("gpsimd", "ten", ten_attn_done)
                if debug:
                    wt("gpsimd", "stm", cnt["stm"])
            if i >= 5:
                wt("gpsimd", "ten", wo_done[i - 5])
            sl = "wps%d" % (i % 5)
            em("gpsimd", lambda g, i=i, sl=sl: g.dma_start(
                out=wob[i % 5][:], in_=wout_d[i]).then_inc(SEM[sl], 16))
            wo_dgp[i] = (sl, bump(sl, 16))
            wo_state["next"] += 1

    wo_fill(5)

    sca_z = {}
    ten_z = 0
    for o in range(16):
        u = o // 4
        for j in range(4):
            wt("tensor", wo_dgp[4 * u + j][0], wo_dgp[4 * u + j][1])
        if o == 0:
            wt("tensor", "vec", vec_attn_done)
        for gr in range(2):
            pn = ["P2", "P3"][gr]
            ps = [P2, P3][gr]
            use_psum("tensor", pn)
            def _zmm(t, o=o, gr=gr, u=u, ps=ps):
                ins = None
                for kd in range(KC):
                    ins = t.matmul(ps[:, 0:512],
                                   wob[(4 * u + kd // 4) % 5][:, 512 * (kd % 4) + 128 * (o % 4):512 * (kd % 4) + 128 * (o % 4) + 128],
                                   O[kd][:, 512 * gr:512 * (gr + 1)],
                                   start=(kd == 0), stop=(kd == KC - 1))
                ins.then_inc(SEM["ten"], 1)
            em("tensor", _zmm)
            tchk = bump("ten")
            wt("scalar", "ten", tchk)
            em("scalar", lambda s, o=o, gr=gr, ps=ps: s.activation(
                zsb[o][:, 512 * gr:512 * (gr + 1)], ps[:], AF.Copy).then_inc(SEM["sca"], 1))
            sca_z[(o, gr)] = bump("sca")
            set_pfree([pn], "sca", sca_z[(o, gr)])
        if o % 4 == 3:
            for j in range(4):
                wo_done[4 * u + j] = cnt["ten"]
            wo_fill(4 * (u + 1) + 5)
        wt("scalar", "sca", sca_z[(o, 1)])
        if o >= 2:
            wt("scalar", "ten", ten_z)
        em("scalar", lambda s, o=o: s.activation(
            sq[o % 2][:, 0:TM], zsb[o][:], AF.Square).then_inc(SEM["sca"], 1))
        schk = bump("sca")
        wt("tensor", "sca", schk)
        if o == 0:
            use_psum("tensor", "P4", "P5", "P0")
        def _zst(t, o=o):
            t.matmul(P4[0:1, 0:512], onesc[:], zsb[o][:, 0:512], start=(o == 0), stop=(o == 15))
            t.matmul(P5[0:1, 0:512], onesc[:], zsb[o][:, 512:1024], start=(o == 0), stop=(o == 15))
            t.matmul(P0[0:1, 0:512], onesc[:], sq[o % 2][:, 0:512], start=(o == 0), stop=(o == 15))
            t.matmul(P0[0:1, 512:1024], onesc[:], sq[o % 2][:, 512:1024],
                     start=(o == 0), stop=(o == 15)).then_inc(SEM["ten"], 1)
        em("tensor", _zst)
        ten_z = bump("ten")

    if debug:
        for o in range(16):
            wt("sync", "sca", sca_z[(o, 1)])
            em("sync", lambda s, o=o: s.dma_start(
                out=dbg["dbg_z"][128 * o:128 * (o + 1), :], in_=zsb[o][:]).then_inc(SEM["out"], 16))
            bump("out", 16)
        wt("gpsimd", "out", cnt["out"])
        em("gpsimd", lambda g: g.memset(consts[0:1, 7:8], 0.0).then_inc(SEM["stm"], 1))
        bump("stm")

    rows2, vec_rep2 = ln_rows(TM, [P4[0:1, 0:512], P5[0:1, 0:512]],
                              [P0[0:1, 0:512], P0[0:1, 512:1024]], ten_z,
                              rowca, rowcb, REPc1, REPc2)
    set_pfree(["P0", "P4", "P5"], "vec", rows2)

    vec_x2 = []
    for i in range(16):
        wt("vector", "vec", vec_rep2)
        wt("vector", "dxm", 256)
        if debug:
            wt("vector", "stm", cnt["stm"])
        def _x2(v, i=i):
            v.tensor_mul(tfa[:], zsb[i][:], REPc1[:, 0:TM])
            v.tensor_sub(tfa[:], tfa[:], REPc2[:, 0:TM])
            v.tensor_add(zsb[i][:], xm[i][:], tfa[:]).then_inc(SEM["vec"], 1)
        em("vector", _x2)
        vec_x2.append(bump("vec"))
    x2 = zsb
    vec_x2_done = cnt["vec"]

    if debug:
        for i in range(16):
            wt("sync", "vec", vec_x2_done)
            em("sync", lambda s, i=i: s.dma_start(
                out=dbg["dbg_x2"][128 * i:128 * (i + 1), :], in_=x2[i][:]).then_inc(SEM["out"], 16))
            bump("out", 16)
        wt("gpsimd", "out", cnt["out"])
        em("gpsimd", lambda g: g.memset(consts[0:1, 7:8], 0.0).then_inc(SEM["stm"], 1))
        bump("stm")

    ten3 = 0
    for i in range(16):
        if i >= 2:
            wt("vector", "ten", ten3)
        em("vector", lambda v, i=i: v.tensor_mul(
            sq[i % 2][:, 0:TM], x2[i][:], x2[i][:]).then_inc(SEM["vec"], 1))
        vchk = bump("vec")
        wt("tensor", "vec", vchk)
        if i == 0:
            use_psum("tensor", "P4", "P5", "P0")
        def _st3(t, i=i):
            t.matmul(P4[0:1, 0:512], onesc[:], x2[i][:, 0:512], start=(i == 0), stop=(i == 15))
            t.matmul(P5[0:1, 0:512], onesc[:], x2[i][:, 512:1024], start=(i == 0), stop=(i == 15))
            t.matmul(P0[0:1, 0:512], onesc[:], sq[i % 2][:, 0:512], start=(i == 0), stop=(i == 15))
            t.matmul(P0[0:1, 512:1024], onesc[:], sq[i % 2][:, 512:1024],
                     start=(i == 0), stop=(i == 15)).then_inc(SEM["ten"], 1)
        em("tensor", _st3)
        ten3 = bump("ten")
    rows3, vec_rep3 = ln_rows(TM, [P4[0:1, 0:512], P5[0:1, 0:512]],
                              [P0[0:1, 0:512], P0[0:1, 512:1024]], ten3,
                              rowca, rowcb, REPc1, REPc2)
    set_pfree(["P0", "P4", "P5"], "vec", rows3)
    ten_ln3_done = cnt["ten"]

    ctx_c1.close()
    g_t = [sb(ctx_d, f"g{i}", [128, TM], BF16) for i in range(HC)]
    silub = sb(ctx_d, "silub", [128, 512], BF16)
    ctx_d2 = ExitStack()
    tq = [sb(ctx_d2, f"tq{i}", [128, 2048], BF16) for i in range(2)]
    y2 = O

    vec_y2 = []
    for i in range(16):
        wt("vector", "vec", vec_rep3)
        if debug:
            wt("vector", "stm", cnt["stm"])
        def _y2(v, i=i):
            v.tensor_mul(y2[i][:], x2[i][:], REPc1[:, 0:TM])
            v.tensor_sub(y2[i][:], y2[i][:], REPc2[:, 0:TM]).then_inc(SEM["vec"], 1)
        em("vector", _y2)
        vec_y2.append(bump("vec"))
    vec_y2_done = cnt["vec"]

    if debug:
        for i in range(16):
            wt("sync", "vec", vec_y2_done)
            em("sync", lambda s, i=i: s.dma_start(
                out=dbg["dbg_y2"][128 * i:128 * (i + 1), :], in_=y2[i][:]).then_inc(SEM["out"], 16))
            bump("out", 16)
        wt("gpsimd", "out", cnt["out"])
        em("gpsimd", lambda g: g.memset(consts[0:1, 7:8], 0.0).then_inc(SEM["stm"], 1))
        bump("stm")
        wt("scalar", "stm", cnt["stm"])

    dsy_w = {}
    sca_q = {}
    vec_q = {}
    ten_h = {}
    sca_sil = {}
    vec_g = {}

    def w13_load(idx):
        if idx >= 2 * HC:
            return
        hb, wi = idx // 2, idx % 2
        dram = w1_d if wi == 0 else w3_d
        qn = "sync" if idx % 2 == 0 else "gpsimd"
        if idx >= 2:
            wt(qn, "sca", sca_q.get(idx - 2, 0))
        if idx <= 1:
            wt(qn, "vec", vec_scan_drain)
        sl = "sts0" if idx % 2 == 0 else "stg1"
        em(qn, lambda s, dram=dram, hb=hb, idx=idx, sl=sl: s.dma_start(
            out=strm[idx % 2][:, 0:2048], in_=dram[hb]).then_inc(SEM[sl], 16))
        dsy_w[idx] = (sl, bump(sl, 16))

    w13_load(0)
    w13_load(1)
    for hb in range(HC):
        for wi in range(2):
            idx = 2 * hb + wi
            coff = 1 if wi == 0 else 3
            wt("scalar", dsy_w[idx][0], dsy_w[idx][1])
            if idx <= 1:
                wt("scalar", "ten", ten_ln3_done)
            if idx >= 2:
                wt("scalar", "ten", ten_h.get(idx - 2, 0))
                wt("scalar", "vec", vec_q.get(idx - 2, 0))
            def _qnt(s, idx=idx, coff=coff):
                s.activation(tq[idx % 2][:, 0:2048], strm[idx % 2][:, 0:2048], AF.Sign,
                             bias=sth[:, coff:coff + 1])
                s.activation(sq[idx % 2][:, 0:2048], strm[idx % 2][:, 0:2048], AF.Sign,
                             bias=sth[:, coff - 1:coff]).then_inc(SEM["sca"], 1)
            em("scalar", _qnt)
            sca_q[idx] = bump("sca")
            wt("vector", "sca", sca_q[idx])
            em("vector", lambda v, idx=idx: v.tensor_add(
                tq[idx % 2][:, 0:2048], tq[idx % 2][:, 0:2048],
                sq[idx % 2][:, 0:2048]).then_inc(SEM["vec"], 1))
            vec_q[idx] = bump("vec")
            w13_load(idx + 2)
            wt("tensor", "vec", vec_q[idx])
            if idx == 0:
                wt("tensor", "vec", vec_y2_done)
            for gr in range(2):
                pn = (["P2", "P3"] if wi == 0 else ["P4", "P5"])[gr]
                ps = ([P2, P3] if wi == 0 else [P4, P5])[gr]
                use_psum("tensor", pn)
                def _hmm(t, idx=idx, gr=gr, ps=ps):
                    ins = None
                    for kd in range(KC):
                        ins = t.matmul(ps[:, 0:512],
                                       tq[idx % 2][:, 128 * kd:128 * (kd + 1)],
                                       y2[kd][:, 512 * gr:512 * (gr + 1)],
                                       start=(kd == 0), stop=(kd == KC - 1))
                    ins.then_inc(SEM["ten"], 1)
                em("tensor", _hmm)
                bump("ten")
            ten_h[idx] = cnt["ten"]
        for gr in range(2):
            wt("scalar", "ten", ten_h[2 * hb + 1])
            if hb > 0 or gr > 0:
                wt("scalar", "vec", vec_g.get((hb, gr - 1) if gr > 0 else (hb - 1, 1), 0))
            pn1 = ["P2", "P3"][gr]
            pn3 = ["P4", "P5"][gr]
            def _sil(s, gr=gr):
                s.activation(silub[:], [P2, P3][gr][:], AF.Silu,
                             scale=sclc[:, 0:1]).then_inc(SEM["sca"], 1)
            em("scalar", _sil)
            sca_sil[(hb, gr)] = bump("sca")
            set_pfree([pn1], "sca", sca_sil[(hb, gr)])
            wt("vector", "sca", sca_sil[(hb, gr)])
            def _gm(v, hb=hb, gr=gr):
                v.tensor_mul(g_t[hb][:, 512 * gr:512 * (gr + 1)], silub[:],
                             [P4, P5][gr][:]).then_inc(SEM["vec"], 1)
            em("vector", _gm)
            vec_g[(hb, gr)] = bump("vec")
            set_pfree([pn3], "vec", vec_g[(hb, gr)])
    vec_g_done = cnt["vec"]
    ctx_d2.close()
    tq2 = [sb(ctx_d, f"tq2_{i}", [128, 1408], BF16) for i in range(2)]
    q2s = sb(ctx_d, "q2s", [128, 1408], BF16)
    z2 = O

    if debug:
        for hb in range(HC):
            wt("sync", "vec", vec_g_done)
            em("sync", lambda s, hb=hb: s.dma_start(
                out=dbg["dbg_g"][128 * hb:128 * (hb + 1), :], in_=g_t[hb][:]).then_inc(SEM["out"], 16))
            bump("out", 16)

    dsy_w2 = {}
    sca_q2 = {}
    vec_q2 = {}
    ten_w2 = {}
    sca_z2 = {}
    ten_z4 = 0
    NHCQ = (11, 11, 11, 10)

    def w2_load(idx):
        if idx >= 64:
            return
        qn = "sync" if idx % 2 == 0 else "gpsimd"
        if idx >= 2:
            wt(qn, "sca", sca_q2.get(idx - 2, 0))
        else:
            wt(qn, "sca", sca_q[2 * HC - 2 + idx])
        wdt = 128 * NHCQ[idx % 4]
        sl = "sts0" if idx % 2 == 0 else "stg1"
        em(qn, lambda s, idx=idx, wdt=wdt, sl=sl: s.dma_start(
            out=strm[idx % 2][:, 0:wdt], in_=w2q_d[idx][:, 0:wdt]).then_inc(SEM[sl], 16))
        dsy_w2[idx] = (sl, bump(sl, 16))

    w2_load(0)
    w2_load(1)
    for ob in range(16):
        for qt in range(4):
            idx = 4 * ob + qt
            wdt = 128 * NHCQ[qt]
            wt("scalar", dsy_w2[idx][0], dsy_w2[idx][1])
            if idx >= 2:
                wt("scalar", "ten", ten_w2.get(idx - 2, 0))
            if idx >= 1:
                wt("scalar", "vec", vec_q2.get(idx - 1, 0))
            def _q2(s, idx=idx, wdt=wdt):
                s.activation(tq2[idx % 2][:, 0:wdt], strm[idx % 2][:, 0:wdt], AF.Sign,
                             bias=sth[:, 5:6])
                s.activation(q2s[:, 0:wdt], strm[idx % 2][:, 0:wdt], AF.Sign,
                             bias=sth[:, 4:5]).then_inc(SEM["sca"], 1)
            em("scalar", _q2)
            sca_q2[idx] = bump("sca")
            wt("vector", "sca", sca_q2[idx])
            em("vector", lambda v, idx=idx, wdt=wdt: v.tensor_add(
                tq2[idx % 2][:, 0:wdt], tq2[idx % 2][:, 0:wdt],
                q2s[:, 0:wdt]).then_inc(SEM["vec"], 1))
            vec_q2[idx] = bump("vec")
            w2_load(idx + 2)
            wt("tensor", "vec", vec_q2[idx])
            if idx == 0:
                wt("tensor", "vec", vec_g_done)
            hc0 = sum(NHCQ[:qt])
            for gr in range(2):
                pn = ["P2", "P3"][gr]
                ps = [P2, P3][gr]
                if qt == 0:
                    use_psum("tensor", pn)
                def _wmm(t, idx=idx, gr=gr, ps=ps, hc0=hc0, qt=qt):
                    ins = None
                    nhc = NHCQ[qt]
                    for j in range(nhc):
                        ins = t.matmul(ps[:, 0:512],
                                       tq2[idx % 2][:, 128 * j:128 * (j + 1)],
                                       g_t[hc0 + j][:, 512 * gr:512 * (gr + 1)],
                                       start=(qt == 0 and j == 0),
                                       stop=(qt == 3 and j == nhc - 1))
                    ins.then_inc(SEM["ten"], 1)
                em("tensor", _wmm)
                bump("ten")
            ten_w2[idx] = cnt["ten"]
        for gr in range(2):
            wt("scalar", "ten", ten_w2[4 * ob + 3])
            pn = ["P2", "P3"][gr]
            def _ez(s, ob=ob, gr=gr):
                s.activation(z2[ob][:, 512 * gr:512 * (gr + 1)], [P2, P3][gr][:],
                             AF.Copy, scale=sclc[:, 1:2])
                s.activation(z2[ob][:, 512 * gr:512 * (gr + 1)],
                             z2[ob][:, 512 * gr:512 * (gr + 1)], AF.Copy,
                             scale=sth[:, 4:5]).then_inc(SEM["sca"], 1)
            em("scalar", _ez)
            sca_z2[(ob, gr)] = bump("sca")
            set_pfree([pn], "sca", sca_z2[(ob, gr)])
        wt("scalar", "sca", sca_z2[(ob, 1)])
        if ob >= 2:
            wt("scalar", "ten", ten_z4)
        em("scalar", lambda s, ob=ob: s.activation(
            sq[ob % 2][:, 0:TM], z2[ob][:], AF.Square).then_inc(SEM["sca"], 1))
        schk = bump("sca")
        wt("tensor", "sca", schk)
        if ob == 0:
            use_psum("tensor", "P4", "P5", "P0")
        def _st4(t, ob=ob):
            t.matmul(P4[0:1, 0:512], onesc[:], z2[ob][:, 0:512], start=(ob == 0), stop=(ob == 15))
            t.matmul(P5[0:1, 0:512], onesc[:], z2[ob][:, 512:1024], start=(ob == 0), stop=(ob == 15))
            t.matmul(P0[0:1, 0:512], onesc[:], sq[ob % 2][:, 0:512], start=(ob == 0), stop=(ob == 15))
            t.matmul(P0[0:1, 512:1024], onesc[:], sq[ob % 2][:, 512:1024],
                     start=(ob == 0), stop=(ob == 15)).then_inc(SEM["ten"], 1)
        em("tensor", _st4)
        ten_z4 = bump("ten")

    if debug:
        for ob in range(16):
            wt("sync", "ten", ten_z4)
            em("sync", lambda s, ob=ob: s.dma_start(
                out=dbg["dbg_z2"][128 * ob:128 * (ob + 1), :], in_=z2[ob][:]).then_inc(SEM["out"], 16))
            bump("out", 16)
    rows4, vec_rep4 = ln_rows(TM, [P4[0:1, 0:512], P5[0:1, 0:512]],
                              [P0[0:1, 0:512], P0[0:1, 512:1024]], ten_z4,
                              rowca, rowcb, REPc1, REPc2)
    set_pfree(["P0", "P4", "P5"], "vec", rows4)

    for i in range(16):
        wt("vector", "vec", vec_rep4)
        tf = [tfa, tfb][i % 2]
        sl = "sto%d" % (i % 2)
        if i >= 2:
            wt("vector", sl, 16 * (i // 2))
        def _x3(v, i=i, tf=tf):
            v.tensor_mul(tf[:], z2[i][:], REPc1[:, 0:TM])
            v.tensor_sub(tf[:], tf[:], REPc2[:, 0:TM])
            v.tensor_add(tf[:], x2[i][:], tf[:])
            v.tensor_copy(sacct[:, 0:1], tf[:, 0:1]).then_inc(SEM["vec"], 1)
        em("vector", _x3)
        vchk = bump("vec")
        wt("sync", "vec", vchk)
        em("sync", lambda s, i=i, tf=tf, sl=sl: s.dma_start(
            out=out_d[128 * i:128 * (i + 1), :], in_=tf[:]).then_inc(SEM[sl], 16))
        bump(sl, 16)
    wt("sync", "sto0", cnt["sto0"])
    wt("sync", "sto1", cnt["sto1"])
    if cnt["out"] > 0:
        wt("sync", "out", cnt["out"])

    block = ctx.enter_context(nc.Block())

    @block.tensor
    def _(e):
        for f in streams["tensor"]:
            f(e)

    @block.vector
    def _(e):
        for f in streams["vector"]:
            f(e)

    @block.scalar
    def _(e):
        for f in streams["scalar"]:
            f(e)

    @block.sync
    def _(e):
        for f in streams["sync"]:
            f(e)

    @block.gpsimd
    def _(e):
        for f in streams["gpsimd"]:
            f(e)

    import re as _re
    actual = {}
    sample = {}
    for ins in nc.all_instructions():
        s = str(ins)
        for m in _re.finditer(r"update:S\[(\w+)\]\+[+=]\??(\d*)", s):
            nm, val = m.group(1), m.group(2)
            inc = int(val) if val else 1
            actual[nm] = actual.get(nm, 0) + inc
            sample.setdefault(nm, s[:160])
    for nm, v in cnt.items():
        a = actual.get(nm, 0)
        if a != v:
            print("AUDIT sample for", nm, ":", sample.get(nm, "NONE"))
        assert a == v, f"sem audit mismatch {nm}: emitted {a} vs counted {v}"
    ctx_d.close()
    ctx_c2.close()
    ctx_o.close()
    ctx.close()
    return nc


def _tile_colblocks(WT, ncol, blk):
    Dd, C = WT.shape
    kc = Dd // 128
    nb = C // blk
    r = WT.reshape(kc, 128, nb, blk).transpose(2, 1, 0, 3).reshape(nb, 128, kc * blk)
    return np.ascontiguousarray(r)


def prepare_inputs(L0_attn_x, L0_attn_w_project, L0_attn_w_out, L0_ffn_w1, L0_ffn_w3, L0_ffn_w2):
    x = np.asarray(L0_attn_x, np.float32)
    wp = np.asarray(L0_attn_w_project, np.float32)
    wo = np.asarray(L0_attn_w_out, np.float32)
    w1 = np.asarray(L0_ffn_w1, np.float32)
    w3 = np.asarray(L0_ffn_w3, np.float32)
    w2 = np.asarray(L0_ffn_w2, np.float32)

    WpT = np.ascontiguousarray(wp.T)
    WoT = np.ascontiguousarray(wo.T)
    w1p = np.zeros((HIDP, D), np.float32)
    w1p[:HID] = w1
    w3p = np.zeros((HIDP, D), np.float32)
    w3p[:HID] = w3
    w2p = np.zeros((D, HIDP), np.float32)
    w2p[:, :HID] = w2
    W1T = np.ascontiguousarray(w1p.T)
    W3T = np.ascontiguousarray(w3p.T)
    W2T = np.ascontiguousarray(w2p.T)

    quarters = []
    for Q in range(4):
        for base in (0, 2048, 4096):
            cols = WpT[:, base + 512 * Q: base + 512 * Q + 512]
            unit = _tile_colblocks(cols, 512, 512)[0]
            quarters.extend(np.split(unit, 4, axis=1))
    wp_t = np.ascontiguousarray(np.stack(quarters))
    wo_units = _tile_colblocks(WoT, D, 512)
    wout_t = np.ascontiguousarray(
        np.stack([q for u in range(4) for q in np.split(wo_units[u], 4, axis=1)]))
    w1_t = _tile_colblocks(W1T, HIDP, 128)
    w3_t = _tile_colblocks(W3T, HIDP, 128)
    w2_blocks = _tile_colblocks(W2T, D, 128)
    w2q = np.zeros((16, 4, 128, 1408), np.float32)
    offs = [0, 1408, 2816, 4224]
    for qt in range(4):
        w = [1408, 1408, 1408, 1280][qt]
        w2q[:, qt, :, 0:w] = w2_blocks[:, :, offs[qt]:offs[qt] + w]
    w2q = np.ascontiguousarray(w2q.reshape(64, 128, 1408))

    bf = ml_dtypes.bfloat16
    ident = np.eye(128, dtype=np.float32).astype(bf)

    in_maps = []
    for c in range(8):
        b, g = c // 2, c % 2
        ordr = [2 * j + g for j in range(8)] + [2 * m + 1 - g for m in range(8)]
        xT = np.ascontiguousarray(x[b].T)
        x_loc = np.ascontiguousarray(
            xT.reshape(D, 16, 128)[:, ordr, :].reshape(D, T))
        tri = np.where(np.arange(128)[None, :] >= np.arange(128)[:, None], 0.0, NEG)
        flat = np.zeros((128, 128), np.float32) if g == 1 else np.full((128, 128), NEG, np.float32)
        masks = np.stack([tri, flat]).astype(bf)
        in_maps.append({
            "x_loc": x_loc, "masks": masks, "ident": ident,
            "wp": wp_t, "wout": wout_t, "w1t": w1_t, "w3t": w3_t,
            "w2q": w2q,
        })
    return in_maps


def assemble_output(results):
    out = np.zeros((4, T, D), np.float32)
    for c in range(8):
        b, g = c // 2, c % 2
        o = np.asarray(results[c]["out"], np.float32)
        blocks = o.reshape(D, 8, 128)
        for j in range(8):
            gb = 2 * j + g
            out[b, 128 * gb:128 * (gb + 1), :] = blocks[:, j, :].T
    return out


def kernel(L0_attn_x, L0_attn_w_project, L0_attn_w_out, L0_ffn_w1, L0_ffn_w3, L0_ffn_w2):
    in_maps = prepare_inputs(L0_attn_x, L0_attn_w_project, L0_attn_w_out,
                             L0_ffn_w1, L0_ffn_w3, L0_ffn_w2)
    nc = build_nc(debug=False)
    res = run_bass_kernel_spmd(nc, in_maps, core_ids=list(range(8)))
    return assemble_output(res.results)
